# version 05
# baseline (speedup 1.0000x reference)
import math
import numpy as np
import concourse.bass as bass
import concourse.mybir as mybir
from concourse.bass_utils import run_bass_kernel_spmd
from concourse.bass_core_rust import IndirectOffsetOnAxis

F32 = mybir.dt.float32
F32R = mybir.dt.float32r
BF16 = mybir.dt.bfloat16
I32 = mybir.dt.int32
U32 = mybir.dt.uint32
OP = mybir.AluOpType
AF = mybir.ActivationFunctionType
AX = mybir.AxisListType

NCORES = 8
B, T, C = 2, 8192, 1024
N = B * T
TPC = N // NCORES
CCH = C // 128
EPS = 1e-5


TC = 512
NTC = TPC // TC


def emit_ln(nc, tag, bufs, sems, start_gate=None):
    s_sq = sems[tag + "_sq"]
    s_mm = sems[tag + "_mm"]
    s_var = sems[tag + "_var"]
    s_rstd = sems[tag + "_rstd"]
    s_row = sems[tag + "_row"]
    s_bc = sems[tag + "_bc"]
    s_xn = sems[tag + "_xn"]
    s_ch = sems[tag + "_chain"]
    s_vf = sems[tag + "_vf"]
    vfc = [0]

    x_sb = bufs["x_sb"]
    xn_sb = bufs["xn_sb"]
    rows = bufs["rows_sb"]
    sq = bufs["sq_sb"]
    tmp = bufs["tmp_sb"]
    ones_col = bufs["ones_col"]
    ones_row = bufs["ones_row"]
    ps_sum = bufs["ps_sum"]
    ps_sq = bufs["ps_sq"]
    ps_r = bufs["ps_r"]
    ps_n = bufs["ps_n"]

    junkf = bufs["junk_f"]

    def vfence(e):
        vfc[0] += 1
        e.memset(junkf[:, 0:32], 0.0).then_inc(s_vf, 1)
        e.wait_ge(s_vf, vfc[0])

    def gate(e):
        if start_gate is not None:
            e.wait_ge(*start_gate)

    def f_scalar(e):
        gate(e)
        for tc in range(NTC):
            t0 = tc * TC
            if tc >= 1:
                e.wait_ge(s_mm, tc)
            for c in range(CCH):
                inst = e.activation(sq[:, c * TC:(c + 1) * TC],
                                    x_sb[:, c * TPC + t0: c * TPC + t0 + TC],
                                    AF.Square)
            inst.then_inc(s_sq, 1)
            e.wait_ge(s_var, tc + 1)
            vrow = rows[0:1, 3 * TPC + t0: 3 * TPC + t0 + TC]
            lrow = rows[0:1, 4 * TPC + t0: 4 * TPC + t0 + TC]
            e.activation(lrow, vrow, AF.Ln, bias=EPS).then_inc(s_ch, 1)
            e.wait_ge(s_ch, tc + 1)
            e.activation(rows[0:1, TPC + t0: TPC + t0 + TC], lrow, AF.Exp,
                         scale=-0.5).then_inc(s_rstd, 1)

    def f_tensor(e):
        gate(e)
        for tc in range(NTC):
            t0 = tc * TC
            par = tc % 2
            psum_s = ps_sum[0:1, par * TC:(par + 1) * TC]
            psum_q = ps_sq[0:1, par * TC:(par + 1) * TC]
            if tc >= 2:
                e.wait_ge(s_var, tc - 1)
            e.wait_ge(s_sq, tc + 1)
            for c in range(CCH):
                e.matmul(psum_s, ones_col[:, 0:1],
                         x_sb[:, c * TPC + t0: c * TPC + t0 + TC],
                         start=(c == 0), stop=(c == CCH - 1))
            for c in range(CCH):
                mmi = e.matmul(psum_q, ones_col[:, 0:1],
                               sq[:, c * TC:(c + 1) * TC],
                               start=(c == 0), stop=(c == CCH - 1))
            mmi.then_inc(s_mm, 1)
            e.wait_ge(s_row, tc + 1)
            if tc >= 2:
                e.wait_ge(s_xn, tc - 1)
            e.matmul(ps_r[:, par * TC:(par + 1) * TC], ones_row[0:1, :],
                     rows[0:1, TPC + t0: TPC + t0 + TC], start=True, stop=True)
            e.matmul(ps_n[:, par * TC:(par + 1) * TC], ones_row[0:1, :],
                     rows[0:1, 2 * TPC + t0: 2 * TPC + t0 + TC],
                     start=True, stop=True).then_inc(s_bc, 1)

    def f_vector(e):
        e.memset(bufs["eps_tile"], EPS)
        gate(e)
        for tc in range(NTC):
            t0 = tc * TC
            par = tc % 2
            e.wait_ge(s_mm, tc + 1)
            mrow = rows[0:1, t0:t0 + TC]
            vrow = rows[0:1, 3 * TPC + t0: 3 * TPC + t0 + TC]
            m2row = rows[0:1, 5 * TPC + t0: 5 * TPC + t0 + TC]
            e.tensor_scalar(mrow, ps_sum[0:1, par * TC:(par + 1) * TC],
                            1.0 / C, None, OP.mult)
            e.tensor_scalar(vrow, ps_sq[0:1, par * TC:(par + 1) * TC],
                            1.0 / C, None, OP.mult)
            vfence(e)
            e.tensor_mul(m2row, mrow, mrow)
            vfence(e)
            e.tensor_sub(vrow, vrow, m2row).then_inc(s_var, 1)
            e.wait_ge(s_rstd, tc + 1)
            nrow = rows[0:1, 2 * TPC + t0: 2 * TPC + t0 + TC]
            e.scalar_tensor_tensor(nrow, mrow, -1.0,
                                   rows[0:1, TPC + t0: TPC + t0 + TC],
                                   OP.mult, OP.mult).then_inc(s_row, 1)
            e.wait_ge(s_bc, tc + 1)
            for c in range(CCH):
                xs = x_sb[:, c * TPC + t0: c * TPC + t0 + TC]
                e.tensor_mul(tmp[:], xs, ps_r[:, par * TC:(par + 1) * TC])
                vfence(e)
                e.tensor_add(xn_sb[:, c * TPC + t0: c * TPC + t0 + TC],
                             tmp[:], ps_n[:, par * TC:(par + 1) * TC])
                vfence(e)
            e.sem_inc(s_xn, 1)

    return {"scalar": f_scalar, "tensor": f_tensor, "vector": f_vector}


def alloc_ln(nc, tag):
    sems = {}
    for s in ("sq", "mm", "var", "rstd", "row", "bc", "xn", "chain", "vf"):
        sems[tag + "_" + s] = nc.alloc_semaphore(tag + "_" + s)
    eps_tile = nc.alloc_sbuf_tensor(tag + "_eps", [128, 1], F32).ap()
    nc.const_aps.register(F32, EPS, eps_tile[:, 0:1])
    bufs = {
        "eps_tile": eps_tile,
        "rows_sb": nc.alloc_sbuf_tensor(tag + "_rows", [1, 6 * TPC], F32).ap(),
        "sq_sb": nc.alloc_sbuf_tensor(tag + "_sq", [128, CCH * TC], F32).ap(),
        "tmp_sb": nc.alloc_sbuf_tensor(tag + "_tmp", [128, TC], F32).ap(),
        "junk_f": nc.alloc_sbuf_tensor(tag + "_junkf", [128, 32], F32).ap(),
        "ps_sum": nc.alloc_psum_tensor(tag + "_pss", [1, 2 * TC], F32).ap(),
        "ps_sq": nc.alloc_psum_tensor(tag + "_psq", [1, 2 * TC], F32).ap(),
        "ps_r": nc.alloc_psum_tensor(tag + "_psr", [128, 2 * TC], F32).ap(),
        "ps_n": nc.alloc_psum_tensor(tag + "_psn", [128, 2 * TC], F32).ap(),
    }
    return bufs, sems


def make_consts(nc):
    out = {}
    out["ones_col"] = nc.inline_tensor(np.ones((128, 1), np.float32), name="c_onescol").ap()
    out["ones_row"] = nc.inline_tensor(np.ones((1, 128), np.float32), name="c_onesrow").ap()
    return out


def build_launch_ln():
    nc = bass.Bass("TRN2", target_bir_lowering=False)
    x = nc.dram_tensor("x_fm", [C, TPC], F32, kind="ExternalInput").ap()
    xn = nc.dram_tensor("xn_fm", [C, TPC], F32R, kind="ExternalOutput").ap()
    cst = make_consts(nc)

    x_sb = nc.alloc_sbuf_tensor("x_sb", [128, CCH * TPC], F32).ap()
    xn_sb = nc.alloc_sbuf_tensor("xn_sb", [128, CCH * TPC], F32R).ap()
    oc = nc.alloc_sbuf_tensor("oc", [128, 1], F32).ap()
    orow = nc.alloc_sbuf_tensor("orow", [1, 128], F32).ap()
    bufs, sems = alloc_ln(nc, "ln")
    bufs.update({"x_sb": x_sb, "xn_sb": xn_sb, "ones_col": oc, "ones_row": orow})
    ld = nc.alloc_semaphore("ld")
    st = nc.alloc_semaphore("st")
    ln = emit_ln(nc, "ln", bufs, sems, start_gate=(ld, 16 * (CCH + 2)))

    with nc.Block() as block:
        @block.sync
        def _(e):
            for c in range(CCH):
                e.dma_start(out=x_sb[:, c * TPC:(c + 1) * TPC],
                            in_=x[c * 128:(c + 1) * 128, :]).then_inc(ld, 16)
            e.dma_start(out=oc[:], in_=cst["ones_col"]).then_inc(ld, 16)
            e.dma_start(out=orow[:], in_=cst["ones_row"]).then_inc(ld, 16)

        @block.scalar
        def _(e):
            ln["scalar"](e)

        @block.tensor
        def _(e):
            ln["tensor"](e)

        @block.vector
        def _(e):
            ln["vector"](e)

        @block.gpsimd
        def _(e):
            e.wait_ge(sems["ln_xn"], NTC)
            for c in range(CCH):
                e.dma_start(out=xn[c * 128:(c + 1) * 128, :],
                            in_=xn_sb[:, c * TPC:(c + 1) * TPC]).then_inc(st, 16)
            e.wait_ge(st, 16 * CCH)
    return nc


def shard_fm(x_btc):
    flat = np.asarray(x_btc).reshape(N, C)
    return [np.ascontiguousarray(flat[o * TPC:(o + 1) * TPC, :].T) for o in range(NCORES)]


def unshard_fm(shards):
    flat = np.concatenate([s.T for s in shards], axis=0)
    return flat.reshape(B, T, C)


def _pc(v):
    return np.ascontiguousarray(np.asarray(v, np.float32).reshape(CCH, 128).T)


def kernel(**inputs):
    from concourse.bass_utils import run_bass_kernel_spmd
    cores = list(range(NCORES))

    x = np.asarray(inputs["L0_attn_u"], np.float32)
    k_full = np.asarray(inputs["L0_attn_k"], np.float32)
    skip = np.asarray(inputs["L0_attn_skip"], np.float32)
    gate_w = np.asarray(inputs["L0_ffn_gate_w"], np.float32)
    w1 = np.ascontiguousarray(np.asarray(inputs["L0_ffn_w1"], np.float32))
    w3 = np.ascontiguousarray(np.asarray(inputs["L0_ffn_w3"], np.float32))
    w2 = np.ascontiguousarray(np.asarray(inputs["L0_ffn_w2"], np.float32))
    e_bias = np.asarray(inputs["L0_ffn_e_bias"], np.float32)
    Wdec = np.asarray(inputs["L1_attn_W"], np.float32).reshape(-1)
    Uv = np.asarray(inputs["L1_attn_U"], np.float32).reshape(-1)
    tmk = np.asarray(inputs["L1_attn_time_mix_k"], np.float32).reshape(-1)
    tmv = np.asarray(inputs["L1_attn_time_mix_v"], np.float32).reshape(-1)
    tmr = np.asarray(inputs["L1_attn_time_mix_r"], np.float32).reshape(-1)
    WkT = np.ascontiguousarray(np.asarray(inputs["L1_attn_Wk"], np.float32).T)
    WvT = np.ascontiguousarray(np.asarray(inputs["L1_attn_Wv"], np.float32).T)
    WrT = np.ascontiguousarray(np.asarray(inputs["L1_attn_Wr"], np.float32).T)
    WoT = np.ascontiguousarray(np.asarray(inputs["L1_attn_Wo"], np.float32).T)
    wgT = np.ascontiguousarray(np.asarray(inputs["L1_ffn_w_g"], np.float32).T)
    c_fc = np.ascontiguousarray(np.asarray(inputs["L1_ffn_c_fc"], np.float32))
    c_proj = np.ascontiguousarray(np.asarray(inputs["L1_ffn_c_proj"], np.float32))

    nc_ln = build_launch_ln()
    xs = shard_fm(x)
    res = run_bass_kernel_spmd(nc_ln, [{"x_fm": xs[o]} for o in cores],
                               core_ids=cores)
    xn0 = [np.asarray(r["xn_fm"]) for r in res.results]

    nc_conv = build_launch_conv()
    xn0_btc = unshard_fm(xn0)
    cxs, cks = host_conv_shards(xn0_btc, k_full)
    res = run_bass_kernel_spmd(nc_conv,
                               [{"xn_cs": cxs[o], "k_cs": cks[o]} for o in cores],
                               core_ids=cores)
    y = host_conv_unshard([np.asarray(r["y_cs"]) for r in res.results])
    ys = shard_fm(y)

    nc_moe0 = build_launch_moe0()
    skip_pc = _pc(skip)
    gwt = np.ascontiguousarray(gate_w.T)
    ebr = e_bias.reshape(1, E0).copy()
    ims = [{"x_fm": xs[o], "xn_fm": xn0[o], "y_fm": ys[o], "skip_pc": skip_pc,
            "gate_wt": gwt, "e_bias_row": ebr, "w1": w1, "w3": w3, "w2": w2}
           for o in cores]
    res = run_bass_kernel_spmd(nc_moe0, ims, core_ids=cores)
    x1s = [np.asarray(r["x1_fm"]) for r in res.results]

    res = run_bass_kernel_spmd(nc_ln, [{"x_fm": x1s[o]} for o in cores],
                               core_ids=cores)
    xn1 = [np.asarray(r["xn_fm"]) for r in res.results]

    nc_wkv = build_launch_wkv1()
    tm_pc = np.concatenate([_pc(tmk), _pc(tmv), _pc(tmr)], axis=1)
    W_pc = _pc(Wdec)
    ims = []
    for o in cores:
        q = o % 4
        xb = np.zeros((C,), np.float32) if q == 0 else xn1[o - 1][:, TPC - 1]
        ims.append({"xn1_fm": xn1[o], "xbound_pc": _pc(xb), "tm_pc": tm_pc,
                    "W_pc": W_pc, "WkT": WkT, "WvT": WvT, "WrT": WrT,
                    "wgT": wgT})
    res = run_bass_kernel_spmd(nc_wkv, ims, core_ids=cores)
    EKs = [np.asarray(r["EK_fm"]) for r in res.results]
    EKVs = [np.asarray(r["EKV_fm"]) for r in res.results]
    Slocs = [np.asarray(r["Sloc_fm"]) for r in res.results]
    Zlocs = [np.asarray(r["Zloc_fm"]) for r in res.results]
    SRs = [np.asarray(r["SR_fm"]) for r in res.results]
    logits_full = np.ascontiguousarray(
        np.concatenate([np.asarray(r["logits_tm"]) for r in res.results], axis=0))

    nc_fin = build_launch_fin()
    endsS = np.ascontiguousarray(np.stack([Slocs[o][:, TPC - 1] for o in cores], axis=1))
    endsZ = np.ascontiguousarray(np.stack([Zlocs[o][:, TPC - 1] for o in cores], axis=1))
    U_pc = _pc(Uv)
    ims = []
    for o in cores:
        b, q = o // 4, o % 4
        m = np.zeros(8, np.float32)
        g = np.zeros(8, np.float32)
        for qp in range(q):
            src = b * 4 + qp
            m[src] = 2048.0 * (q - 1 - qp)
            g[src] = 1.0
        ims.append({"x1_fm": x1s[o], "xn1_fm": xn1[o], "EK_fm": EKs[o],
                    "EKV_fm": EKVs[o], "Sloc_fm": Slocs[o], "Zloc_fm": Zlocs[o],
                    "SR_fm": SRs[o], "endsS": endsS, "endsZ": endsZ,
                    "mrow": m.reshape(1, 8), "grow": g.reshape(1, 8),
                    "W_pc": W_pc, "U_pc": U_pc, "WoT": WoT,
                    "logits_full": logits_full, "c_fc": c_fc, "c_proj": c_proj})
    res = run_bass_kernel_spmd(nc_fin, ims, core_ids=cores)
    x2 = unshard_fm([np.asarray(r["x2_fm"]) for r in res.results])
    return np.ascontiguousarray(x2.astype(np.float32))


FCH = 16
GF = 8192


def _conv_consts():
    n1g = np.arange(64)
    k1g = np.arange(128)
    n2g = np.arange(128)
    k2g = np.arange(128)
    ang1 = 2 * np.pi * np.outer(n1g, k1g) / 128.0
    D1r = np.cos(ang1).astype(np.float32)
    D1i = (-np.sin(ang1)).astype(np.float32)
    angT = 2 * np.pi * np.outer(k1g, n2g) / 16384.0
    Twr = np.cos(angT).astype(np.float32)
    Twi = (-np.sin(angT)).astype(np.float32)
    ang2 = 2 * np.pi * np.outer(n2g, k2g) / 128.0
    D2r = np.cos(ang2).astype(np.float32)
    D2i = (-np.sin(ang2)).astype(np.float32)
    angE = 2 * np.pi * np.outer(k2g, n2g) / 128.0
    Er = np.cos(angE).astype(np.float32)
    Ei = np.sin(angE).astype(np.float32)
    angT2 = 2 * np.pi * np.outer(n2g, k1g) / 16384.0
    T2r = np.cos(angT2).astype(np.float32)
    T2i = np.sin(angT2).astype(np.float32)
    angF = 2 * np.pi * np.outer(k1g, n1g) / 128.0
    Fr = (np.cos(angF) / 16384.0).astype(np.float32)
    Fi = (np.sin(angF) / 16384.0).astype(np.float32)
    return {
        "D1r": D1r, "D1i": D1i,
        "D2r": D2r, "D2i": D2i, "nD2i": -D2i,
        "Er": Er, "Ei": Ei, "nEi": -Ei,
        "Fr": Fr, "nFi": -Fi,
        "Twr": Twr, "Twi": Twi, "T2r": T2r, "T2i": T2i,
        "ident": np.eye(128, dtype=np.float32),
    }


def build_launch_conv():
    nc = bass.Bass("TRN2", target_bir_lowering=False)
    xn_cs = nc.dram_tensor("xn_cs", [B, T, 128], F32R, kind="ExternalInput").ap()
    k_cs = nc.dram_tensor("k_cs", [128, T], F32R, kind="ExternalInput").ap()
    y_cs = nc.dram_tensor("y_cs", [B, 64, 128, 128], F32R, kind="ExternalOutput").ap()
    kspec = nc.dram_tensor("kspec", [2, 2, 128, GF], F32)

    cnp = _conv_consts()
    cdr = {n: nc.inline_tensor(v, name="c_" + n).ap() for n, v in cnp.items()}

    mmc = ("D1r", "D1i", "D2r", "nD2i", "D2i", "Er", "nEi", "Ei", "Fr", "nFi")
    csb = {}
    for n in mmc:
        sh = list(cnp[n].shape)
        csb[n] = nc.alloc_sbuf_tensor("s_" + n, sh, F32R).ap()
    for n in ("Twr", "Twi", "T2r", "T2i", "ident"):
        csb[n] = nc.alloc_sbuf_tensor("s_" + n, list(cnp[n].shape), F32).ap()
    Xg = nc.alloc_sbuf_tensor("Xg", [64, GF], F32R).ap()
    Pf_r = nc.alloc_sbuf_tensor("Pf_r", [128, GF], F32).ap()
    Pf_i = nc.alloc_sbuf_tensor("Pf_i", [128, GF], F32).ap()
    Pr_r = nc.alloc_sbuf_tensor("Pr_r", [128, GF], F32R).ap()
    Pr_i = nc.alloc_sbuf_tensor("Pr_i", [128, GF], F32R).ap()
    Cb_r = nc.alloc_sbuf_tensor("Cb_r", [128, 2 * TC], F32).ap()
    Cb_i = nc.alloc_sbuf_tensor("Cb_i", [128, 2 * TC], F32).ap()
    t1 = nc.alloc_sbuf_tensor("t1", [128, TC], F32).ap()
    t2 = nc.alloc_sbuf_tensor("t2", [128, TC], F32).ap()
    junkc = nc.alloc_sbuf_tensor("junkc", [128, 32], F32).ap()
    ps = [nc.alloc_psum_tensor("pp%d" % i, [128, TC], F32).ap() for i in range(8)]

    s_ld = nc.alloc_semaphore("s_ld")
    s_ldg = nc.alloc_semaphore("s_ldg")
    s_lc = [nc.alloc_semaphore("s_lc0"), nc.alloc_semaphore("s_lc1")]
    s_stk = [nc.alloc_semaphore("s_stk0"), nc.alloc_semaphore("s_stk1")]
    s_mm = nc.alloc_semaphore("s_mm")
    s_ev = nc.alloc_semaphore("s_ev")
    s_st = nc.alloc_semaphore("s_st")
    s_cf = nc.alloc_semaphore("s_cf")
    cfc = [0]
    cnt = {"ld": 0, "ldg": 0, "mm": 0, "ev": 0, "st": 0, "lc0": 0, "lc1": 0, "stk0": 0, "stk1": 0}
    prog = {"sync": [], "tensor": [], "vector": [], "scalar": [], "gpsimd": []}

    def em(eng, fn):
        prog[eng].append(fn)

    def cfence(e):
        cfc[0] += 1
        e.memset(junkc[:, 0:32], 0.0).then_inc(s_cf, 1)
        e.wait_ge(s_cf, cfc[0])

    for n in mmc:
        em("gpsimd", lambda e, n=n: e.dma_start(out=csb[n], in_=cdr[n]).then_inc(s_ldg, 16))
        cnt["ldg"] += 16
    for n in ("Twr", "Twi", "T2r", "T2i", "ident"):
        em("sync", lambda e, n=n: e.dma_start(out=csb[n], in_=cdr[n]).then_inc(s_ld, 16))
        cnt["ld"] += 16
    LD_CONST = cnt["ld"]
    LDG_CONST = cnt["ldg"]

    xv = xn_cs.rearrange("b (n1 n2) ch -> b n1 n2 ch", n1=64)
    kv = k_cs.rearrange("ch (n1 n2) -> n1 ch n2", n1=64)

    def stage(nch, pe_fn, dve_fn, pe_pre=None, dve_pre=None):
        mm_base = cnt["mm"]
        ev_base = cnt["ev"]

        def f_pe(e, nch=nch, mm_base=mm_base, ev_base=ev_base, pe_fn=pe_fn, pe_pre=pe_pre):
            e.wait_ge(s_ev, ev_base)
            if pe_pre is not None:
                pe_pre(e)
            for f in range(nch):
                if f >= 2:
                    e.wait_ge(s_ev, ev_base + f - 1)
                inst = pe_fn(e, f, (f % 2) * 2)
                inst.then_inc(s_mm, 1)

        def f_dve(e, nch=nch, mm_base=mm_base, dve_fn=dve_fn, dve_pre=dve_pre):
            if dve_pre is not None:
                dve_pre(e)
            for f in range(nch):
                e.wait_ge(s_mm, mm_base + f + 1)
                inst = dve_fn(e, f, (f % 2) * 2)
                inst.then_inc(s_ev, 1)

        em("tensor", f_pe)
        em("vector", f_dve)
        cnt["mm"] += nch
        cnt["ev"] += nch

    def tw_slice(name, f, uorder):
        c = csb[name]
        if uorder:
            return c[:, 8 * f:8 * (f + 1)].unsqueeze(2).broadcast_to([128, 8, 64])
        else:
            return c[:, :].unsqueeze(1).broadcast_to([128, 4, 128])

    def emit_group(kind, b, h):
        uorder = (kind == "u")
        ld_t = cnt["ld"] + 16
        ev_need = cnt["ev"]

        st_need0 = cnt["st"]

        def f_ld(e, ld_t=ld_t, b=b, h=h, uorder=uorder, st_need0=st_need0):
            e.wait_ge(s_st, st_need0)
            if uorder:
                src = xv[b, :, :, h * 64:(h + 1) * 64]
                dst = Xg[:].rearrange("p (n2 c) -> p n2 c", c=64)
            else:
                src = kv[:, h * 64:(h + 1) * 64, :]
                dst = Xg[:].rearrange("p (c n2) -> p c n2", n2=128)
            e.wait_ge(s_ev, ev_need)
            e.dma_start(out=dst, in_=src).then_inc(s_ld, 16)
        em("sync", f_ld)
        cnt["ld"] += 16

        ld_all = cnt["ld"]

        def pe_s1(e, f, pb):
            sl = Xg[:, f * TC:(f + 1) * TC]
            e.matmul(ps[pb][:], csb["D1r"], sl, start=True, stop=True)
            return e.matmul(ps[pb + 1][:], csb["D1i"], sl, start=True, stop=True)

        def dve_s1(e, f, pb):
            cfence(e)
            pr3 = ps[pb][:].rearrange("p (a b) -> p a b", a=8 if uorder else 4)
            pi3 = ps[pb + 1][:].rearrange("p (a b) -> p a b", a=8 if uorder else 4)
            t13 = t1[:].rearrange("p (a b) -> p a b", a=8 if uorder else 4)
            t23 = t2[:].rearrange("p (a b) -> p a b", a=8 if uorder else 4)
            twr = tw_slice("Twr", f, uorder)
            twi = tw_slice("Twi", f, uorder)
            e.tensor_tensor(t13, pr3, twr, OP.mult)
            e.tensor_tensor(t23, pi3, twi, OP.mult)
            cfence(e)
            e.tensor_sub(Pf_r[:, f * TC:(f + 1) * TC], t1[:], t2[:])
            cfence(e)
            e.tensor_tensor(t13, pr3, twi, OP.mult)
            e.tensor_tensor(t23, pi3, twr, OP.mult)
            cfence(e)
            return e.tensor_add(Pf_i[:, f * TC:(f + 1) * TC], t1[:], t2[:])

        stage(FCH, pe_s1, dve_s1,
              pe_pre=lambda e, ld_all=ld_all: (e.wait_ge(s_ld, ld_all),
                                               e.wait_ge(s_ldg, LDG_CONST)))

        if uorder:
            Afr = Pf_r[:].rearrange("p (n2 c) -> p c n2", c=64)
            Afi = Pf_i[:].rearrange("p (n2 c) -> p c n2", c=64)
        else:
            Afr = Pf_r[:].rearrange("p (c n2) -> p c n2", n2=128)
            Afi = Pf_i[:].rearrange("p (c n2) -> p c n2", n2=128)

        def pe_turn(e, f, pb, Afr=Afr, Afi=Afi):
            ri, q = f % 2, f // 2
            src = Afr if ri == 0 else Afi
            for j in range(4):
                inst = e.transpose(ps[pb][:, j * 128:(j + 1) * 128],
                                   src[:, 4 * q + j, :], csb["ident"])
            return inst

        def dve_turn(e, f, pb):
            ri, q = f % 2, f // 2
            dst = (Pr_r if ri == 0 else Pr_i)[:, q * TC:(q + 1) * TC]
            return e.tensor_copy(dst, ps[pb][:])

        stage(32, pe_turn, dve_turn)

        def pe_s2(e, f, pb):
            br = Pr_r[:, f * TC:(f + 1) * TC]
            bi = Pr_i[:, f * TC:(f + 1) * TC]
            e.matmul(ps[pb][:], csb["D2r"], br, start=True, stop=False)
            e.matmul(ps[pb][:], csb["nD2i"], bi, start=False, stop=True)
            e.matmul(ps[pb + 1][:], csb["D2i"], br, start=True, stop=False)
            return e.matmul(ps[pb + 1][:], csb["D2r"], bi, start=False, stop=True)

        if uorder:
            ldc_base = cnt["ld"]
            evp_base = cnt["ev"]
            st_need = cnt["st"]

            lc_b0 = cnt["lc0"]
            lc_b1 = cnt["lc1"]

            def f_ldc(e, h=h, evp_base=evp_base, st_need=st_need,
                      stk_t0=cnt["stk0"], stk_t1=cnt["stk1"]):
                e.wait_ge(s_st, st_need)
                e.wait_ge(s_stk[0], stk_t0)
                e.wait_ge(s_stk[1], stk_t1)
                for f in range(FCH):
                    sl = (f % 2) * TC
                    if f >= 2:
                        e.wait_ge(s_ev, evp_base + f - 1)
                    e.dma_start(out=Cb_r[:, sl:sl + TC],
                                in_=kspec.ap()[h, 0][:, f * TC:(f + 1) * TC]
                                ).then_inc(s_lc[f % 2], 16)
                    e.dma_start(out=Cb_i[:, sl:sl + TC],
                                in_=kspec.ap()[h, 1][:, f * TC:(f + 1) * TC]
                                ).then_inc(s_lc[f % 2], 16)
            em("sync", f_ldc)
            cnt["lc0"] += 32 * FCH // 2
            cnt["lc1"] += 32 * FCH // 2

            def dve_s2(e, f, pb, lc_b0=lc_b0, lc_b1=lc_b1):
                cfence(e)
                e.wait_ge(s_lc[f % 2], (lc_b0 if f % 2 == 0 else lc_b1)
                          + 32 * (f // 2 + 1))
                sl = (f % 2) * TC
                cr = Cb_r[:, sl:sl + TC]
                ci = Cb_i[:, sl:sl + TC]
                e.tensor_tensor(t1[:], ps[pb][:], cr, OP.mult)
                e.tensor_tensor(t2[:], ps[pb + 1][:], ci, OP.mult)
                cfence(e)
                e.tensor_sub(Pr_r[:, f * TC:(f + 1) * TC], t1[:], t2[:])
                cfence(e)
                e.tensor_tensor(t1[:], ps[pb][:], ci, OP.mult)
                e.tensor_tensor(t2[:], ps[pb + 1][:], cr, OP.mult)
                cfence(e)
                return e.tensor_add(Pr_i[:, f * TC:(f + 1) * TC], t1[:], t2[:])
        else:
            stk_b0 = cnt["stk0"]
            stk_b1 = cnt["stk1"]

            def dve_s2(e, f, pb, stk_b0=stk_b0, stk_b1=stk_b1):
                if f >= 2:
                    e.wait_ge(s_stk[f % 2], (stk_b0 if f % 2 == 0 else stk_b1)
                              + 32 * ((f - 2) // 2 + 1))
                else:
                    e.wait_ge(s_stk[f % 2], stk_b0 if f % 2 == 0 else stk_b1)
                sl = (f % 2) * TC
                e.tensor_copy(Cb_r[:, sl:sl + TC], ps[pb][:])
                return e.tensor_copy(Cb_i[:, sl:sl + TC], ps[pb + 1][:])

        ev_base_s2 = cnt["ev"]
        stage(FCH, pe_s2, dve_s2)

        if not uorder:
            def f_stk(e, h=h, ev_base_s2=ev_base_s2):
                for f in range(FCH):
                    e.wait_ge(s_ev, ev_base_s2 + f + 1)
                    sl = (f % 2) * TC
                    e.dma_start(out=kspec.ap()[h, 0][:, f * TC:(f + 1) * TC],
                                in_=Cb_r[:, sl:sl + TC]).then_inc(s_stk[f % 2], 16)
                    e.dma_start(out=kspec.ap()[h, 1][:, f * TC:(f + 1) * TC],
                                in_=Cb_i[:, sl:sl + TC]).then_inc(s_stk[f % 2], 16)
                e.nop().then_inc(s_st, 16 * 2 * FCH)
            em("scalar", f_stk)
            cnt["stk0"] += 32 * FCH // 2
            cnt["stk1"] += 32 * FCH // 2
            cnt["st"] += 32 * FCH
            return

        def pe_ia(e, f, pb):
            yr = Pr_r[:, f * TC:(f + 1) * TC]
            yi = Pr_i[:, f * TC:(f + 1) * TC]
            e.matmul(ps[pb][:], csb["Er"], yr, start=True, stop=False)
            e.matmul(ps[pb][:], csb["nEi"], yi, start=False, stop=True)
            e.matmul(ps[pb + 1][:], csb["Ei"], yr, start=True, stop=False)
            return e.matmul(ps[pb + 1][:], csb["Er"], yi, start=False, stop=True)

        def dve_ia(e, f, pb):
            cfence(e)
            pr3 = ps[pb][:].rearrange("p (a b) -> p a b", a=4)
            pi3 = ps[pb + 1][:].rearrange("p (a b) -> p a b", a=4)
            t13 = t1[:].rearrange("p (a b) -> p a b", a=4)
            t23 = t2[:].rearrange("p (a b) -> p a b", a=4)
            twr = csb["T2r"][:, :].unsqueeze(1).broadcast_to([128, 4, 128])
            twi = csb["T2i"][:, :].unsqueeze(1).broadcast_to([128, 4, 128])
            e.tensor_tensor(t13, pr3, twr, OP.mult)
            e.tensor_tensor(t23, pi3, twi, OP.mult)
            cfence(e)
            e.tensor_sub(Pf_r[:, f * TC:(f + 1) * TC], t1[:], t2[:])
            cfence(e)
            e.tensor_tensor(t13, pr3, twi, OP.mult)
            e.tensor_tensor(t23, pi3, twr, OP.mult)
            cfence(e)
            return e.tensor_add(Pf_i[:, f * TC:(f + 1) * TC], t1[:], t2[:])

        stage(FCH, pe_ia, dve_ia)

        Af2r = Pf_r[:].rearrange("p (c k1) -> p c k1", c=64)
        Af2i = Pf_i[:].rearrange("p (c k1) -> p c k1", c=64)

        def pe_t2(e, f, pb):
            ri, q = f % 2, f // 2
            src = Af2r if ri == 0 else Af2i
            for j in range(4):
                inst = e.transpose(ps[pb][:, j * 128:(j + 1) * 128],
                                   src[:, 4 * q + j, :], csb["ident"])
            return inst

        def dve_t2(e, f, pb):
            ri, q = f % 2, f // 2
            dst = (Pr_r if ri == 0 else Pr_i)[:, q * TC:(q + 1) * TC]
            return e.tensor_copy(dst, ps[pb][:])

        stage(32, pe_t2, dve_t2)

        def pe_ib(e, f, pb):
            e.matmul(ps[pb][0:64, :], csb["Fr"], Pr_r[:, f * TC:(f + 1) * TC],
                     start=True, stop=False)
            return e.matmul(ps[pb][0:64, :], csb["nFi"], Pr_i[:, f * TC:(f + 1) * TC],
                            start=False, stop=True)

        def dve_ib(e, f, pb):
            return e.tensor_copy(Xg[0:64, f * TC:(f + 1) * TC], ps[pb][0:64, :])

        stage(FCH, pe_ib, dve_ib)

        ev_need3 = cnt["ev"]

        def f_sty(e, b=b, h=h, ev_need3=ev_need3):
            e.wait_ge(s_ev, ev_need3)
            dst = y_cs[b, :, h * 64:(h + 1) * 64, :]
            src = Xg[0:64, :].rearrange("p (c n2) -> p c n2", n2=128)
            e.dma_start(out=dst, in_=src).then_inc(s_st, 16)
        em("scalar", f_sty)
        cnt["st"] += 16

    emit_group("k", 0, 0)
    emit_group("k", 0, 1)
    for b in range(B):
        for h in range(2):
            emit_group("u", b, h)

    st_total = cnt["st"]
    em("scalar", lambda e: e.wait_ge(s_st, st_total))

    with nc.Block() as block:
        @block.sync
        def _(e):
            for fn in prog["sync"]:
                fn(e)

        @block.tensor
        def _(e):
            for fn in prog["tensor"]:
                fn(e)

        @block.vector
        def _(e):
            for fn in prog["vector"]:
                fn(e)

        @block.scalar
        def _(e):
            for fn in prog["scalar"]:
                fn(e)

        @block.gpsimd
        def _(e):
            for fn in prog["gpsimd"]:
                fn(e)
    return nc


def host_conv_shards(xn_btc, k_full):
    xs, ks = [], []
    for o in range(NCORES):
        sl = slice(o * 128, (o + 1) * 128)
        xs.append(np.ascontiguousarray(np.asarray(xn_btc)[:, :, sl]))
        ks.append(np.ascontiguousarray(np.asarray(k_full)[sl, :]))
    return xs, ks


def host_conv_unshard(y_shards):
    y = np.zeros((B, T, C), np.float32)
    for o, ys in enumerate(y_shards):
        yy = np.asarray(ys).transpose(0, 1, 3, 2).reshape(B, T, 128)
        y[:, :, o * 128:(o + 1) * 128] = yy
    return y


E0 = 32
I0 = 768
ICH = I0 // 128
HTC = 1024
NHC = TPC // HTC
HNT = HTC // TC


def build_launch_moe0(debug=False):
    nc = bass.Bass("TRN2", target_bir_lowering=False)
    x_d = nc.dram_tensor("x_fm", [C, TPC], F32, kind="ExternalInput").ap()
    xn_d = nc.dram_tensor("xn_fm", [C, TPC], F32R, kind="ExternalInput").ap()
    y_d = nc.dram_tensor("y_fm", [C, TPC], F32, kind="ExternalInput").ap()
    skip_d = nc.dram_tensor("skip_pc", [128, CCH], F32, kind="ExternalInput").ap()
    gwt_d = nc.dram_tensor("gate_wt", [C, E0], F32R, kind="ExternalInput").ap()
    eb_d = nc.dram_tensor("e_bias_row", [1, E0], F32R, kind="ExternalInput").ap()
    w1_d = nc.dram_tensor("w1", [E0, C, I0], F32R, kind="ExternalInput").ap()
    w3_d = nc.dram_tensor("w3", [E0, C, I0], F32R, kind="ExternalInput").ap()
    w2_d = nc.dram_tensor("w2", [E0, I0, C], F32R, kind="ExternalInput").ap()
    x1_d = nc.dram_tensor("x1_fm", [C, TPC], F32, kind="ExternalOutput").ap()
    if debug:
        dbg_sc = nc.dram_tensor("dbg_sc", [E0, HTC], F32, kind="ExternalOutput").ap()
        dbg_cb = nc.dram_tensor("dbg_cb", [E0, HTC], F32, kind="ExternalOutput").ap()
        dbg_crep = nc.dram_tensor("dbg_crep", [128, HTC], F32, kind="ExternalOutput").ap()
        dbg_sga = nc.dram_tensor("dbg_sga", [128, ICH * TC], F32, kind="ExternalOutput").ap()
        dbg_cb0 = nc.dram_tensor("dbg_cb0", [E0, HTC], F32, kind="ExternalOutput").ap()
        cb0_sb = nc.alloc_sbuf_tensor("cb0_sb", [E0, HTC], F32).ap()

    selmat = np.zeros((E0, E0 * 128), np.float32)
    for e in range(E0):
        selmat[e, e * 128:(e + 1) * 128] = 1.0
    sel_d = nc.inline_tensor(selmat, name="c_sel").ap()
    onesr_d = nc.inline_tensor(np.ones((1, 128), np.float32), name="c_onesr").ap()

    x_sb = nc.alloc_sbuf_tensor("x_sb", [128, CCH * HTC], F32).ap()
    xn_sb = nc.alloc_sbuf_tensor("xn_sb", [128, CCH * HTC], F32R).ap()
    y_sb = nc.alloc_sbuf_tensor("y_sb", [128, 2 * TC], F32).ap()
    skip_sb = nc.alloc_sbuf_tensor("skip_sb", [128, CCH], F32).ap()
    gwt_sb = nc.alloc_sbuf_tensor("gwt_sb", [128, CCH * E0], F32R).ap()
    sel_sb = nc.alloc_sbuf_tensor("sel_sb", [E0, E0 * 128], F32R).ap()
    onesr_sb = nc.alloc_sbuf_tensor("onesr_sb", [1, 128], F32R).ap()
    ebr_sb = nc.alloc_sbuf_tensor("ebr_sb", [128, E0], F32).ap()
    sc_sb = nc.alloc_sbuf_tensor("sc_sb", [128, (HTC // 128) * E0], F32).ap()
    cmb_tm = nc.alloc_sbuf_tensor("cmb_tm", [128, (HTC // 128) * E0], F32R).ap()
    junk_v = nc.alloc_sbuf_tensor("junk_v", [128, 64], F32).ap()
    junk_big = nc.alloc_sbuf_tensor("junk_big", [128, 1024], F32).ap()
    junk_a = nc.alloc_sbuf_tensor("junk_a", [128, 64], U32).ap()
    afc0 = [0]
    comb_sb = nc.alloc_sbuf_tensor("comb_sb", [E0, HTC], F32R).ap()
    blk_sb = nc.alloc_sbuf_tensor("blk_sb", [128, E0], F32).ap()
    bb_sb = nc.alloc_sbuf_tensor("bb_sb", [128, E0], F32).ap()
    msk_sb = nc.alloc_sbuf_tensor("msk_sb", [128, 4 * E0], F32).ap()
    scol_sb = nc.alloc_sbuf_tensor("scol_sb", [128, 16], F32).ap()
    crep_sb = nc.alloc_sbuf_tensor("crep_sb", [128, HTC], F32).ap()
    w1_sb = nc.alloc_sbuf_tensor("w1_sb", [128, CCH * ICH * 128], F32R).ap()
    w3_sb = nc.alloc_sbuf_tensor("w3_sb", [128, CCH * ICH * 128], F32R).ap()
    w2_sb = nc.alloc_sbuf_tensor("w2_sb", [128, ICH * CCH * 128], F32R).ap()
    sga_sb = nc.alloc_sbuf_tensor("sga_sb", [128, ICH * TC], F32R).ap()
    tmp_sb = nc.alloc_sbuf_tensor("tmp_sb", [128, TC], F32).ap()
    tmp2_sb = nc.alloc_sbuf_tensor("tmp2_sb", [128, TC], F32).ap()

    pg = [nc.alloc_psum_tensor("pg%d" % i, [128, TC], F32).ap() for i in range(2)]
    pu = [nc.alloc_psum_tensor("pu%d" % i, [128, TC], F32).ap() for i in range(2)]
    po = [nc.alloc_psum_tensor("po%d" % i, [128, TC], F32).ap() for i in range(2)]
    pr_ps = nc.alloc_psum_tensor("pr_ps", [128, 128], F32).ap()
    pt_ps = nc.alloc_psum_tensor("pt_ps", [128, 256], F32).ap()
    pt_ps2 = pt_ps[0:E0, 128:256]
    onesident = nc.alloc_sbuf_tensor("ident_moe", [128, 128], F32).ap()
    ident_r = nc.alloc_sbuf_tensor("ident_r", [128, 128], F32R).ap()
    ident_d2 = nc.inline_tensor(np.eye(128, dtype=np.float32), name="c_ident_moe").ap()

    s_ld = nc.alloc_semaphore("s_ld")
    s_lcst = nc.alloc_semaphore("s_lcst")
    s_ldg = nc.alloc_semaphore("s_ldg")
    s_ldy = [nc.alloc_semaphore("s_ldy0"), nc.alloc_semaphore("s_ldy1")]
    s_w = nc.alloc_semaphore("s_w")
    s_mm = nc.alloc_semaphore("s_mm")
    s_ev = nc.alloc_semaphore("s_ev")
    s_act = nc.alloc_semaphore("s_act")
    s_st = nc.alloc_semaphore("s_st")
    s_red = nc.alloc_semaphore("s_red")
    red_n = [0]
    cnt = {"ld": 0, "ldg": 0, "mm": 0, "ev": 0, "act": 0, "st": 0}
    prog = {"sync": [], "tensor": [], "vector": [], "scalar": [], "gpsimd": []}

    def em(eng, fn):
        prog[eng].append(fn)

    em("sync", lambda e: e.dma_start(out=skip_sb[:], in_=skip_d).then_inc(s_lcst, 16))
    em("sync", lambda e: e.dma_start(out=gwt_sb[:].rearrange("p (c e) -> p c e", e=E0),
                                     in_=gwt_d.rearrange("(c p) e -> p c e", p=128)).then_inc(s_lcst, 16))
    em("gpsimd", lambda e: e.dma_start(out=onesr_sb[:], in_=onesr_d).then_inc(s_ldg, 16))
    em("gpsimd", lambda e: e.dma_start(out=ident_r[:], in_=ident_d2).then_inc(s_ldg, 16))
    em("gpsimd", lambda e: e.dma_start(out=sel_sb[:], in_=sel_d).then_inc(s_ldg, 16))
    cnt["ldg"] += 48
    eb_ld = None

    def f_ldeb(e):
        e.dma_start(out=nc_ebrow, in_=eb_d).then_inc(s_lcst, 16)
    nc_ebrow = nc.alloc_sbuf_tensor("ebrow", [1, E0], F32R).ap()
    em("sync", f_ldeb)
    em("sync", lambda e: e.dma_start(out=onesident[:], in_=ident_d2).then_inc(s_lcst, 16))
    LCST = 64

    def f_ebrep_pe(e, LDG=cnt["ldg"]):
        e.wait_ge(s_lcst, LCST)
        e.wait_ge(s_ldg, LDG)
        e.matmul(pt_ps[:, 0:E0], onesr_sb[:], nc_ebrow, start=True, stop=True).then_inc(s_mm, 1)
    em("tensor", f_ebrep_pe)
    cnt["mm"] += 1
    mm_eb = cnt["mm"]

    def f_ebrep_dve(e):
        e.wait_ge(s_mm, mm_eb)
        e.tensor_copy(ebr_sb[:], pt_ps[:, 0:E0]).then_inc(s_ev, 1)
    em("vector", f_ebrep_dve)
    cnt["ev"] += 1
    EV_EBR = cnt["ev"]

    for half in range(NHC):
        t0g = half * HTC
        st_pre = cnt["st"]

        def f_ldh(e, t0g=t0g, st_pre=st_pre):
            e.wait_ge(s_st, st_pre)
            e.wait_ge(s_lcst, LCST)
            for c in range(CCH):
                e.dma_start(out=x_sb[:, c * HTC:(c + 1) * HTC],
                            in_=x_d[c * 128:(c + 1) * 128, t0g:t0g + HTC]).then_inc(s_ld, 16)
                e.dma_start(out=xn_sb[:, c * HTC:(c + 1) * HTC],
                            in_=xn_d[c * 128:(c + 1) * 128, t0g:t0g + HTC]).then_inc(s_ld, 16)
        em("sync", f_ldh)
        cnt["ld"] += 16 * 2 * CCH
        ld_h = cnt["ld"]

        ev_res = cnt["ev"]
        
        ldy_b0 = cnt.get("ldy0", 0)
        ldy_b1 = cnt.get("ldy1", 0)

        def f_ldy(e, t0g=t0g, ld_h=ld_h, ev_res=ev_res):
            e.wait_ge(s_ld, ld_h)
            for j in range(CCH * HNT):
                c, tc = j // HNT, j % HNT
                if j >= 2:
                    e.wait_ge(s_ev, ev_res + j - 1)
                e.dma_start(out=y_sb[:, (j % 2) * TC:(j % 2 + 1) * TC],
                            in_=y_d[c * 128:(c + 1) * 128,
                                    t0g + tc * TC:t0g + (tc + 1) * TC]
                            ).then_inc(s_ldy[j % 2], 16)
        em("sync", f_ldy)
        cnt["ldy0"] = ldy_b0 + 16 * CCH * HNT // 2
        cnt["ldy1"] = ldy_b1 + 16 * CCH * HNT // 2

        def f_res(e, ld_h=ld_h, ldy_b0=ldy_b0, ldy_b1=ldy_b1):
            e.wait_ge(s_ld, ld_h)
            for j in range(CCH * HNT):
                c, tc = j // HNT, j % HNT
                sl = slice(c * HTC + tc * TC, c * HTC + (tc + 1) * TC)
                e.wait_ge(s_ldy[j % 2], (ldy_b0 if j % 2 == 0 else ldy_b1)
                          + 16 * (j // 2 + 1))
                e.scalar_tensor_tensor(tmp_sb[:], xn_sb[:, sl],
                                       skip_sb[:, c:c + 1],
                                       y_sb[:, (j % 2) * TC:(j % 2 + 1) * TC],
                                       OP.mult, OP.add)
                e.memset(junk_v[:, 0:32], 0.0).then_inc(s_red, 1)
                red_n[0] += 1
                e.wait_ge(s_red, red_n[0])
                e.tensor_add(x_sb[:, sl], x_sb[:, sl], tmp_sb[:]).then_inc(s_ev, 1)
                red_n[0] += 1
                e.memset(junk_v[:, 0:32], 0.0).then_inc(s_red, 1)
                e.wait_ge(s_red, red_n[0])
        em("vector", f_res)
        cnt["ev"] += CCH * HNT

        NBLK = HTC // 128
        mm_rtr = cnt["mm"]

        def f_rtr_pe(e, ld_h=ld_h, mm_rtr=mm_rtr, act_base=cnt["act"]):
            e.wait_ge(s_ld, ld_h)
            e.wait_ge(s_ev, EV_EBR)
            rbanks = [pr_ps, pt_ps]
            for blk in range(NBLK):
                par = blk % 2
                if blk >= 2:
                    e.wait_ge(s_act, act_base + blk - 1)
                for c in range(CCH):
                    inst = e.matmul(rbanks[par][:, 0:E0],
                                    xn_sb[:, c * HTC + blk * 128: c * HTC + (blk + 1) * 128],
                                    gwt_sb[:, c * E0:(c + 1) * E0],
                                    start=(c == 0), stop=(c == CCH - 1))
                inst.then_inc(s_mm, 1)
        em("tensor", f_rtr_pe)
        cnt["mm"] += NBLK

        def f_rtr_act(e, mm_rtr=mm_rtr):
            rbanks = [pr_ps, pt_ps]
            for blk in range(NBLK):
                e.wait_ge(s_mm, mm_rtr + blk + 1)
                e.activation(sc_sb[:, blk * E0:(blk + 1) * E0],
                             rbanks[blk % 2][:, 0:E0],
                             AF.Sigmoid).then_inc(s_act, 1)
            afc0[0] += 1
            e.memzero(junk_a[:, afc0[0] % 64:afc0[0] % 64 + 1]).then_inc(s_act, 1)
        em("scalar", f_rtr_act)
        cnt["act"] += NBLK + 1
        act_rtr_done = cnt["act"]

        ev_t4 = cnt["ev"]
        mm_t4 = cnt["mm"]

        def f_t4_dve(e, ev_t4=ev_t4, mm_t4=mm_t4, act_rtr_done=act_rtr_done):
            def mf(e):
                red_n[0] += 1
                e.memset(junk_v[:, 0:32], 0.0).then_inc(s_red, 1)
                e.wait_ge(s_red, red_n[0])
            e.wait_ge(s_act, act_rtr_done)
            for blk in range(NBLK):
                scb = sc_sb[:, blk * E0:(blk + 1) * E0]
                e.tensor_add(bb_sb[:], scb, ebr_sb[:])
                mf(e)
                for k in range(4):
                    mcol = scol_sb[:, k:k + 1]
                    e.tensor_reduce(mcol, bb_sb[:], AX.X, OP.max)
                    mf(e)
                    e.tensor_tensor(msk_sb[:, k * E0:(k + 1) * E0], bb_sb[:],
                                    mcol.broadcast_to([128, E0]), OP.is_ge)
                    mf(e)
                    e.tensor_mul(tmp_sb[:, 0:E0], msk_sb[:, k * E0:(k + 1) * E0], scb)
                    mf(e)
                    e.tensor_reduce(scol_sb[:, 8 + k:9 + k], tmp_sb[:, 0:E0],
                                    AX.X, OP.add)
                    if k < 3:
                        mf(e)
                        e.tensor_scalar(tmp_sb[:, 0:E0], msk_sb[:, k * E0:(k + 1) * E0],
                                        1e30, None, OP.mult)
                        mf(e)
                        e.tensor_sub(bb_sb[:], bb_sb[:], tmp_sb[:, 0:E0])
                        mf(e)
                mf(e)
                for k in range(4):
                    if k == 0:
                        e.tensor_tensor(cmb_tm[:, blk * E0:(blk + 1) * E0],
                                        msk_sb[:, 0:E0],
                                        scol_sb[:, 8:9].broadcast_to([128, E0]), OP.mult)
                        mf(e)
                    else:
                        e.tensor_tensor(tmp_sb[:, 0:E0], msk_sb[:, k * E0:(k + 1) * E0],
                                        scol_sb[:, 8 + k:9 + k].broadcast_to([128, E0]),
                                        OP.mult)
                        mf(e)
                        e.tensor_add(cmb_tm[:, blk * E0:(blk + 1) * E0],
                                     cmb_tm[:, blk * E0:(blk + 1) * E0], tmp_sb[:, 0:E0])
                        mf(e)
                e.tensor_reduce(scol_sb[:, 14:15], cmb_tm[:, blk * E0:(blk + 1) * E0],
                                AX.X, OP.add)
                mf(e)
                e.reciprocal(scol_sb[:, 15:16], scol_sb[:, 14:15])
                mf(e)
                e.tensor_tensor(cmb_tm[:, blk * E0:(blk + 1) * E0],
                                cmb_tm[:, blk * E0:(blk + 1) * E0],
                                scol_sb[:, 15:16].broadcast_to([128, E0]), OP.mult)
            e.memset(junk_v[:], 0.0).then_inc(s_ev, 1)
        em("vector", f_t4_dve)
        cnt["ev"] += 1
        ev_cmb_tm = cnt["ev"]

        def f_tt_pe(e, ev_cmb_tm=ev_cmb_tm, ld_h=ld_h):
            e.wait_ge(s_ev, ev_cmb_tm)
            banks = [pg[0], pg[1], pu[0], pu[1], po[0], po[1], pt_ps, pr_ps]
            for blk in range(NBLK):
                inst = e.matmul(banks[blk][0:E0, 0:128],
                                cmb_tm[:, blk * E0:(blk + 1) * E0], ident_r[:],
                                start=True, stop=True)
            inst.then_inc(s_mm, 1)
        em("tensor", f_tt_pe)
        cnt["mm"] += 1
        mm_tt = cnt["mm"]

        def f_tt_dve(e, mm_tt=mm_tt):
            e.wait_ge(s_mm, mm_tt)
            e.memset(junk_v[:, 32:64], 0.0)
            banks = [pg[0], pg[1], pu[0], pu[1], po[0], po[1], pt_ps, pr_ps]
            for blk in range(NBLK):
                e.tensor_copy(comb_sb[0:E0, blk * 128:(blk + 1) * 128],
                              banks[blk][0:E0, 0:128])
            e.memset(junk_v[:, 0:32], 0.0).then_inc(s_ev, 1)
        em("vector", f_tt_dve)
        cnt["ev"] += 1

        for ex in range(E0):
            mm_prev = cnt["mm"]

            def f_ldw(e, ex=ex, mm_prev=mm_prev):
                e.wait_ge(s_mm, mm_prev)
                e.dma_start(out=w1_sb[:].rearrange("p (hc ic j) -> p hc ic j", hc=CCH, ic=ICH),
                            in_=w1_d[ex].rearrange("(hc p) (ic j) -> p hc ic j", p=128, j=128)
                            ).then_inc(s_w, 16)
                e.dma_start(out=w3_sb[:].rearrange("p (hc ic j) -> p hc ic j", hc=CCH, ic=ICH),
                            in_=w3_d[ex].rearrange("(hc p) (ic j) -> p hc ic j", p=128, j=128)
                            ).then_inc(s_w, 16)
                e.dma_start(out=w2_sb[:].rearrange("p (ic hc j) -> p ic hc j", ic=ICH, hc=CCH),
                            in_=w2_d[ex].rearrange("(ic p) (hc j) -> p ic hc j", p=128, j=128)
                            ).then_inc(s_w, 16)
            em("sync", f_ldw)
            cnt["w"] = cnt.get("w", 0) + 48
            ld_w = cnt["w"]

            ev_comb = cnt["ev"]

            def f_crep_pe(e, ex=ex, ld_w=ld_w, ev_prev=cnt["ev"]):
                e.wait_ge(s_ev, ev_prev)
                for tc in range(HNT):
                    e.matmul(po[tc][:], sel_sb[:, ex * 128:(ex + 1) * 128],
                             comb_sb[0:E0, tc * TC:(tc + 1) * TC],
                             start=True, stop=True).then_inc(s_mm, 1)
                e.wait_ge(s_w, ld_w)
            em("tensor", f_crep_pe)
            mm_crep = cnt["mm"]
            cnt["mm"] += HNT

            def f_crep_dve(e, mm_crep=mm_crep):
                for tc in range(HNT):
                    e.wait_ge(s_mm, mm_crep + tc + 1)
                    e.tensor_copy(crep_sb[:, tc * TC:(tc + 1) * TC], po[tc][:]).then_inc(s_ev, 1)
            em("vector", f_crep_dve)
            ev_crep = cnt["ev"]
            cnt["ev"] += HNT

            for tc in range(HNT):
                tsl = slice(tc * TC, (tc + 1) * TC)
                mm_g0 = cnt["mm"]
                act_g0 = cnt["act"]
                ev_a0 = cnt["ev"]

                def f_exp_pe(e, tc=tc, mm_g0=mm_g0, act_g0=act_g0, ev_a0=ev_a0,
                             ev_crep=ev_crep):
                    for i in range(ICH):
                        if i >= 2:
                            e.wait_ge(s_act, act_g0 + i - 1)
                            e.wait_ge(s_ev, ev_a0 + i - 1)
                        else:
                            e.wait_ge(s_act, max(0, act_g0 + i - 1))
                            e.wait_ge(s_ev, max(0, ev_a0 - CCH + i + 1))
                        for c in range(CCH):
                            e.matmul(pg[i % 2][:],
                                     w1_sb[:, (c * ICH + i) * 128:(c * ICH + i + 1) * 128],
                                     xn_sb[:, c * HTC + tc * TC: c * HTC + (tc + 1) * TC],
                                     start=(c == 0), stop=(c == CCH - 1))
                        inst = None
                        for c in range(CCH):
                            inst = e.matmul(pu[i % 2][:],
                                            w3_sb[:, (c * ICH + i) * 128:(c * ICH + i + 1) * 128],
                                            xn_sb[:, c * HTC + tc * TC: c * HTC + (tc + 1) * TC],
                                            start=(c == 0), stop=(c == CCH - 1))
                        inst.then_inc(s_mm, 1)
                    e.wait_ge(s_ev, ev_a0 + ICH + 1)
                    for hc in range(CCH):
                        if hc >= 2:
                            e.wait_ge(s_ev, ev_a0 + ICH + 1 + hc - 1)
                        for i in range(ICH):
                            inst = e.matmul(po[hc % 2][:],
                                            w2_sb[:, (i * CCH + hc) * 128:(i * CCH + hc + 1) * 128],
                                            sga_sb[:, i * TC:(i + 1) * TC],
                                            start=(i == 0), stop=(i == ICH - 1))
                        inst.then_inc(s_mm, 1)
                em("tensor", f_exp_pe)
                mm_gs = cnt["mm"]
                cnt["mm"] += ICH + CCH

                def f_exp_act(e, mm_g0=mm_g0):
                    for i in range(ICH):
                        e.wait_ge(s_mm, mm_g0 + i + 1)
                        e.activation(sga_sb[:, i * TC:(i + 1) * TC], pg[i % 2][:],
                                     AF.Silu).then_inc(s_act, 1)
                em("scalar", f_exp_act)
                cnt["act"] += ICH

                def f_exp_dve(e, tc=tc, mm_g0=mm_g0, act_g0=act_g0, ev_crep=ev_crep):
                    pass_holder = None
                    for i in range(ICH):
                        e.wait_ge(s_act, act_g0 + i + 1)
                        e.wait_ge(s_mm, mm_g0 + i + 1)
                        e.tensor_mul(tmp_sb[:], sga_sb[:, i * TC:(i + 1) * TC], pu[i % 2][:])
                        red_n[0] += 1
                        e.memset(junk_v[:, 0:32], 0.0).then_inc(s_red, 1)
                        e.wait_ge(s_red, red_n[0])
                        e.tensor_mul(sga_sb[:, i * TC:(i + 1) * TC], tmp_sb[:],
                                     crep_sb[:, tc * TC:(tc + 1) * TC]).then_inc(s_ev, 1)
                        red_n[0] += 1
                        e.memset(junk_v[:, 0:32], 0.0).then_inc(s_red, 1)
                        e.wait_ge(s_red, red_n[0])
                    e.memset(junk_v[:, 32:64], 0.0).then_inc(s_ev, 1)
                    for hc in range(CCH):
                        e.wait_ge(s_mm, mm_g0 + ICH + hc + 1)
                        sl = slice(hc * HTC + tc * TC, hc * HTC + (tc + 1) * TC)
                        e.tensor_add(x_sb[:, sl], x_sb[:, sl], po[hc % 2][:]).then_inc(s_ev, 1)
                em("vector", f_exp_dve)
                cnt["ev"] += ICH + 1 + CCH

        def f_xfence(e):
            e.memset(junk_v[:], 0.0).then_inc(s_ev, 1)
        em("vector", f_xfence)
        cnt["ev"] += 1
        ev_done = cnt["ev"]

        def f_st(e, t0g=t0g, ev_done=ev_done):
            e.wait_ge(s_ev, ev_done)
            for c in range(CCH):
                e.dma_start(out=x1_d[c * 128:(c + 1) * 128, t0g:t0g + HTC],
                            in_=x_sb[:, c * HTC:(c + 1) * HTC]).then_inc(s_st, 16)
        em("gpsimd", f_st)
        cnt["st"] += 16 * CCH

    if debug:
        ev_fin = cnt["ev"]

        def f_dbg(e, ev_fin=ev_fin):
            e.wait_ge(s_ev, ev_fin)
            e.dma_start(out=dbg_sc, in_=sc_sb[0:E0, :]).then_inc(s_st, 16)
            e.dma_start(out=dbg_cb, in_=comb_sb[0:E0, :]).then_inc(s_st, 16)
            e.dma_start(out=dbg_crep, in_=crep_sb[:]).then_inc(s_st, 16)
            e.dma_start(out=dbg_sga, in_=sga_sb[:]).then_inc(s_st, 16)
            e.dma_start(out=dbg_cb0, in_=cb0_sb[0:E0, :]).then_inc(s_st, 16)
        em("gpsimd", f_dbg)
        cnt["st"] += 80

    st_total = cnt["st"]
    em("gpsimd", lambda e: e.wait_ge(s_st, st_total))

    with nc.Block() as block:
        @block.sync
        def _(e):
            for fn in prog["sync"]:
                fn(e)

        @block.tensor
        def _(e):
            for fn in prog["tensor"]:
                fn(e)

        @block.vector
        def _(e):
            for fn in prog["vector"]:
                fn(e)

        @block.scalar
        def _(e):
            for fn in prog["scalar"]:
                fn(e)

        @block.gpsimd
        def _(e):
            for fn in prog["gpsimd"]:
                fn(e)
    return nc


def build_launch_wkv1():
    nc = bass.Bass("TRN2", target_bir_lowering=False)
    xn1_d = nc.dram_tensor("xn1_fm", [C, TPC], F32R, kind="ExternalInput").ap()
    xb_d = nc.dram_tensor("xbound_pc", [128, CCH], F32, kind="ExternalInput").ap()
    tm_d = nc.dram_tensor("tm_pc", [128, 3 * CCH], F32, kind="ExternalInput").ap()
    W_d = nc.dram_tensor("W_pc", [128, CCH], F32, kind="ExternalInput").ap()
    wk_d = nc.dram_tensor("WkT", [C, C], F32R, kind="ExternalInput").ap()
    wv_d = nc.dram_tensor("WvT", [C, C], F32R, kind="ExternalInput").ap()
    wr_d = nc.dram_tensor("WrT", [C, C], F32R, kind="ExternalInput").ap()
    wg_d = nc.dram_tensor("wgT", [C, 8], F32R, kind="ExternalInput").ap()
    ek_d = nc.dram_tensor("EK_fm", [C, TPC], F32, kind="ExternalOutput").ap()
    ekv_d = nc.dram_tensor("EKV_fm", [C, TPC], F32, kind="ExternalOutput").ap()
    sl_d = nc.dram_tensor("Sloc_fm", [C, TPC], F32, kind="ExternalOutput").ap()
    zl_d = nc.dram_tensor("Zloc_fm", [C, TPC], F32, kind="ExternalOutput").ap()
    sr_d = nc.dram_tensor("SR_fm", [C, TPC], F32, kind="ExternalOutput").ap()
    lg_d = nc.dram_tensor("logits_tm", [TPC, 8], F32, kind="ExternalOutput").ap()

    xn1_sb = nc.alloc_sbuf_tensor("xn1_sb", [128, CCH * HTC], F32R).ap()
    mix_sb = nc.alloc_sbuf_tensor("mix_sb", [128, CCH * HTC], F32R).ap()
    wmat_sb = nc.alloc_sbuf_tensor("wmat_sb", [128, CCH * CCH * 128], F32R).ap()
    ekh_sb = nc.alloc_sbuf_tensor("ekh_sb", [128, CCH * HTC], F32).ap()
    pc_sb = nc.alloc_sbuf_tensor("pc_sb", [128, 8 * CCH], F32).ap()
    wg_sb = nc.alloc_sbuf_tensor("wg_sb", [128, CCH * 8], F32R).ap()
    ones_t = nc.alloc_sbuf_tensor("ones_t", [128, HTC], F32).ap()
    arep_sb = nc.alloc_sbuf_tensor("arep_sb", [128, HTC], F32).ap()
    vt_sb = nc.alloc_sbuf_tensor("vt_sb", [128, HTC], F32).ap()
    ekvt_sb = nc.alloc_sbuf_tensor("ekvt_sb", [128, HTC], F32).ap()
    scanS = nc.alloc_sbuf_tensor("scanS", [128, HTC + 1], F32).ap()
    scanZ = nc.alloc_sbuf_tensor("scanZ", [128, HTC + 1], F32).ap()
    seed_sb = nc.alloc_sbuf_tensor("seed_sb", [128, 2 * CCH], F32).ap()
    xlast_sb = nc.alloc_sbuf_tensor("xlast_sb", [128, CCH], F32).ap()
    acol_sb = nc.alloc_sbuf_tensor("acol_sb", [128, CCH], F32).ap()
    tm1_sb = nc.alloc_sbuf_tensor("tm1_sb", [128, 3 * CCH], F32).ap()
    tmpd = nc.alloc_sbuf_tensor("tmpd", [128, HTC], F32).ap()
    lgst_sb = nc.alloc_sbuf_tensor("lgst_sb", [128, (HTC // 128) * 8], F32).ap()
    junk_v = nc.alloc_sbuf_tensor("junkd_v", [128, 64], F32).ap()
    junk_a2 = nc.alloc_sbuf_tensor("junkd_a", [128, 4], U32).ap()

    pk = [nc.alloc_psum_tensor("pk%d" % i, [128, TC], F32).ap() for i in range(4)]
    pl = [nc.alloc_psum_tensor("pl0", [128, 8], F32).ap(),
          nc.alloc_psum_tensor("pl1", [128, 8], F32).ap()]

    s_ld = nc.alloc_semaphore("d_ld")
    s_cst = nc.alloc_semaphore("d_cst")
    s_dch = nc.alloc_semaphore("d_dch")
    dch = [0]
    s_lw = nc.alloc_semaphore("d_lw")
    s_mm = nc.alloc_semaphore("d_mm")
    s_ev = nc.alloc_semaphore("d_ev")
    s_act = nc.alloc_semaphore("d_act")
    s_st = nc.alloc_semaphore("d_st")
    cnt = {"ld": 0, "lw": 0, "mm": 0, "ev": 0, "act": 0, "st": 0}
    prog = {"sync": [], "tensor": [], "vector": [], "scalar": [], "gpsimd": []}

    def em(eng, fn):
        prog[eng].append(fn)

    def f_ld0(e):
        e.dma_start(out=pc_sb[:, 0:3 * CCH], in_=tm_d).then_inc(s_cst, 16)
        e.dma_start(out=pc_sb[:, 3 * CCH:4 * CCH], in_=W_d).then_inc(s_cst, 16)
        e.dma_start(out=pc_sb[:, 4 * CCH:5 * CCH], in_=xb_d).then_inc(s_cst, 16)
        e.dma_start(out=wg_sb[:].rearrange("p (c e) -> p c e", e=8),
                    in_=wg_d.rearrange("(c p) e -> p c e", p=128)).then_inc(s_cst, 16)
    em("sync", f_ld0)
    LD0 = 64

    def f_prep_v(e):
        e.wait_ge(s_cst, LD0)
        e.memset(ones_t[:], 1.0)
        e.tensor_scalar(tm1_sb[:], pc_sb[:, 0:3 * CCH], -1.0, 1.0, OP.mult, OP.add)
        e.memset(junk_v[:], 0.0).then_inc(s_ev, 1)
    em("vector", f_prep_v)
    cnt["ev"] += 1
    EV_PREP = cnt["ev"]

    s_chain = nc.alloc_semaphore("d_chain")

    def f_prep_a(e):
        e.memzero(junk_a2[:])
        e.wait_ge(s_cst, LD0)
        e.activation(pc_sb[:, 5 * CCH:6 * CCH], pc_sb[:, 3 * CCH:4 * CCH],
                     AF.Exp).then_inc(s_chain, 1)
        e.wait_ge(s_chain, 1)
        e.activation(acol_sb[:], pc_sb[:, 5 * CCH:6 * CCH], AF.Exp, scale=-1.0)
        e.copy(junk_a2[:, 0:1], junk_a2[:, 0:1]).then_inc(s_act, 1)
    em("scalar", f_prep_a)
    cnt["act"] += 1
    ACT_PREP = cnt["act"]

    HB = HTC

    for half in range(NHC):
        t0g = half * HB
        st_pre = cnt["st"]

        ev_save_prev = cnt["ev"]

        def f_ldh(e, t0g=t0g, st_pre=st_pre, ev_save_prev=ev_save_prev):
            e.wait_ge(s_st, st_pre)
            e.wait_ge(s_ev, ev_save_prev)
            e.wait_ge(s_cst, LD0)
            for c in range(CCH):
                e.dma_start(out=xn1_sb[:, c * HB:(c + 1) * HB],
                            in_=xn1_d[c * 128:(c + 1) * 128, t0g:t0g + HB]).then_inc(s_ld, 16)
        em("sync", f_ldh)
        cnt["ld"] += 16 * CCH
        ld_h = cnt["ld"]

        ev_sv = cnt["ev"]

        def f_sv(e, ld_h=ld_h, half=half):
            e.wait_ge(s_ld, ld_h)
            e.wait_ge(s_ev, EV_PREP)
            if half == 0:
                for c in range(CCH):
                    e.tensor_copy(xlast_sb[:, c:c + 1],
                                  pc_sb[:, 4 * CCH + c:4 * CCH + c + 1])
            e.memset(junk_v[:], 0.0).then_inc(s_ev, 1)
        em("vector", f_sv)
        cnt["ev"] += 1

        for phase, (wd, tmoff, out_kind) in enumerate(
                (("k", 0, "ek"), ("v", 1, "ekv"), ("r", 2, "sr"))):
            wten = {"k": wk_d, "v": wv_d, "r": wr_d}[wd]
            mm_gate = cnt["mm"]

            def f_lw(e, wten=wten, mm_gate=mm_gate):
                e.wait_ge(s_mm, mm_gate)
                e.dma_start(out=wmat_sb[:].rearrange("p (c d j) -> p c d j", c=CCH, d=CCH),
                            in_=wten.rearrange("(c p) (d j) -> p c d j", p=128, j=128)
                            ).then_inc(s_lw, 16)
            em("sync", f_lw)
            cnt["lw"] += 16
            lw_t = cnt["lw"]

            ev_mix0 = cnt["ev"]

            def f_mix(e, tmoff=tmoff, ld_h=ld_h, half=half, ev_sv=ev_sv,
                      mm_gate=mm_gate):
                e.wait_ge(s_ld, ld_h)
                e.wait_ge(s_ev, ev_sv + 1)
                e.wait_ge(s_mm, mm_gate)
                for c in range(CCH):
                    tmcol = pc_sb[:, tmoff * CCH + c:tmoff * CCH + c + 1]
                    t1col = tm1_sb[:, tmoff * CCH + c:tmoff * CCH + c + 1]
                    xc = xn1_sb[:, c * HB:(c + 1) * HB]
                    mc = mix_sb[:, c * HB:(c + 1) * HB]
                    e.tensor_scalar(tmpd[:], xc, tmcol, None, OP.mult).then_inc(s_dch, 1)
                    dch[0] += 1
                    e.wait_ge(s_dch, dch[0])
                    e.scalar_tensor_tensor(mc[:, 1:HB], xc[:, 0:HB - 1], t1col,
                                           tmpd[:, 1:HB], OP.mult, OP.add)
                    e.scalar_tensor_tensor(mc[:, 0:1], xlast_sb[:, c:c + 1], t1col,
                                           tmpd[:, 0:1], OP.mult, OP.add).then_inc(s_dch, 1)
                    dch[0] += 1
                    e.wait_ge(s_dch, dch[0])
                e.memset(junk_v[:], 0.0).then_inc(s_ev, 1)
            em("vector", f_mix)
            cnt["ev"] += 1
            ev_mix = cnt["ev"]

            if wd == "r":
                def f_savelast(e, ld_h=ld_h, ev_mix=ev_mix):
                    e.wait_ge(s_ev, ev_mix)
                    for c in range(CCH):
                        e.tensor_copy(xlast_sb[:, c:c + 1],
                                      xn1_sb[:, c * HB + HB - 1:c * HB + HB])
                    e.memset(junk_v[:], 0.0).then_inc(s_ev, 1)
                em("vector", f_savelast)
                cnt["ev"] += 1

            mm_p0 = cnt["mm"]
            HNT2 = HB // TC
            act_p0 = cnt["act"]

            def f_proj_pe(e, lw_t=lw_t, ev_mix=ev_mix, mm_p0=mm_p0, wd=wd,
                          act_p0=act_p0):
                e.wait_ge(s_lw, lw_t)
                e.wait_ge(s_ev, ev_mix)
                fence = 1 if wd in ("v", "r") else 0
                n = 0
                for j in range(CCH):
                    for tc in range(HNT2):
                        pb = pk[n % 4]
                        if n >= 4:
                            np_, jp, tcp = n - 4, (n - 4) // HNT2, (n - 4) % HNT2
                            e.wait_ge(s_act,
                                      act_p0 + jp * (HNT2 + fence) + tcp + 1)
                        else:
                            e.wait_ge(s_act, act_p0)
                        for c in range(CCH):
                            inst = e.matmul(pb[:],
                                            wmat_sb[:, (c * CCH + j) * 128:(c * CCH + j + 1) * 128],
                                            mix_sb[:, c * HB + tc * TC:c * HB + (tc + 1) * TC],
                                            start=(c == 0), stop=(c == CCH - 1))
                        inst.then_inc(s_mm, 1)
                        n += 1
            em("tensor", f_proj_pe)
            cnt["mm"] += CCH * HNT2

            scan_ev_base = cnt["ev"]

            def f_proj_act(e, mm_p0=mm_p0, wd=wd, scan_ev_base=scan_ev_base,
                           st_base_r=cnt["st"]):
                n = 0
                for j in range(CCH):
                    if wd == "v" and j >= 1:
                        e.wait_ge(s_ev, scan_ev_base + 3 * (j - 1) + 1)
                    if wd == "r" and j >= 1:
                        e.wait_ge(s_st, st_base_r + 16 * j)
                    for tc in range(HNT2):
                        e.wait_ge(s_mm, mm_p0 + n + 1)
                        pb = pk[n % 4]
                        if wd == "k":
                            e.activation(ekh_sb[:, j * HB + tc * TC:j * HB + (tc + 1) * TC],
                                         pb[:], AF.Exp).then_inc(s_act, 1)
                        elif wd == "v":
                            e.activation(vt_sb[:, tc * TC:(tc + 1) * TC], pb[:],
                                         AF.Copy).then_inc(s_act, 1)
                        else:
                            e.activation(vt_sb[:, tc * TC:(tc + 1) * TC], pb[:],
                                         AF.Sigmoid).then_inc(s_act, 1)
                        n += 1
                        if wd in ("v", "r") and tc == HNT2 - 1:
                            e.copy(junk_a2[:, 0:1], junk_a2[:, 0:1]).then_inc(s_act, 1)
            em("scalar", f_proj_act)
            cnt["act"] += CCH * HNT2 + (CCH if wd in ("v", "r") else 0)
            act_done = cnt["act"]

            if wd == "k":
                def f_stek(e, act_done=act_done, t0g=t0g):
                    e.wait_ge(s_act, act_done)
                    for c in range(CCH):
                        e.dma_start(out=ek_d[c * 128:(c + 1) * 128, t0g:t0g + HB],
                                    in_=ekh_sb[:, c * HB:(c + 1) * HB]).then_inc(s_st, 16)
                em("gpsimd", f_stek)
                cnt["st"] += 16 * CCH
            elif wd == "v":
                act_v0 = act_p0

                def f_scan_v(e, mm_p0=mm_p0, act_v0=act_v0, half=half, t0g=t0g,
                             ev_base=cnt["ev"], st_base=cnt["st"]):
                    for j in range(CCH):
                        e.wait_ge(s_act, act_v0 + (j + 1) * (HNT2 + 1))
                        e.wait_ge(s_act, ACT_PREP)
                        if j >= 1:
                            e.wait_ge(s_st, st_base + 16 * 3 * j)
                        e.tensor_mul(ekvt_sb[:], ekh_sb[:, j * HB:(j + 1) * HB], vt_sb[:])
                        e.tensor_scalar(arep_sb[:], ones_t[:],
                                        acol_sb[:, j:j + 1], None, OP.mult)
                        if half == 0:
                            e.memset(scanS[:, 0:1], 0.0)
                            e.memset(scanZ[:, 0:1], 0.0)
                        else:
                            e.tensor_copy(scanS[:, 0:1], seed_sb[:, 2 * j:2 * j + 1])
                            e.tensor_copy(scanZ[:, 0:1], seed_sb[:, 2 * j + 1:2 * j + 2])
                        e.memset(junk_v[:], 0.0).then_inc(s_ev, 1)
                        e.wait_ge(s_ev, ev_base + 3 * j + 1)
                        e.tensor_tensor_scan(scanS[:, 1:HB + 1], arep_sb[:], ekvt_sb[:],
                                             scanS[:, 0:1], OP.mult, OP.add)
                        e.tensor_tensor_scan(scanZ[:, 1:HB + 1], arep_sb[:], ekh_sb[:, j * HB:(j + 1) * HB],
                                             scanZ[:, 0:1], OP.mult, OP.add)
                        e.memset(junk_v[:], 0.0).then_inc(s_ev, 1)
                        e.wait_ge(s_ev, ev_base + 3 * j + 2)
                        e.tensor_copy(seed_sb[:, 2 * j:2 * j + 1], scanS[:, HB:HB + 1])
                        e.tensor_copy(seed_sb[:, 2 * j + 1:2 * j + 2], scanZ[:, HB:HB + 1])
                        e.memset(junk_v[:], 0.0).then_inc(s_ev, 1)
                em("vector", f_scan_v)
                ev_scan0 = cnt["ev"]
                cnt["ev"] += 3 * CCH

                def f_stv(e, ev_scan0=ev_scan0, t0g=t0g):
                    for j in range(CCH):
                        e.wait_ge(s_ev, ev_scan0 + 3 * j + 3)
                        sl = slice(j * 128, (j + 1) * 128)
                        e.dma_start(out=ekv_d[sl, t0g:t0g + HB], in_=ekvt_sb[:]).then_inc(s_st, 16)
                        e.dma_start(out=sl_d[sl, t0g:t0g + HB], in_=scanS[:, 1:HB + 1]).then_inc(s_st, 16)
                        e.dma_start(out=zl_d[sl, t0g:t0g + HB], in_=scanZ[:, 1:HB + 1]).then_inc(s_st, 16)
                em("gpsimd", f_stv)
                cnt["st"] += 16 * 3 * CCH
            else:
                def f_str(e, act_p0=act_p0, t0g=t0g, st_base=cnt["st"]):
                    for j in range(CCH):
                        e.wait_ge(s_act, act_p0 + (j + 1) * (HNT2 + 1))
                        if j >= 1:
                            e.wait_ge(s_st, st_base + 16 * j)
                        e.dma_start(out=sr_d[j * 128:(j + 1) * 128, t0g:t0g + HB],
                                    in_=vt_sb[:]).then_inc(s_st, 16)
                em("gpsimd", f_str)
                cnt["st"] += 16 * CCH

            if wd == "v":
                pass

        mm_l0 = cnt["mm"]
        NTK = HB // 128

        def f_lg_pe(e, ld_h=ld_h, mm_l0=mm_l0, ev_gate=cnt["ev"], act_gate=cnt["act"]):
            e.wait_ge(s_act, act_gate)
            e.wait_ge(s_cst, LD0)
            for i in range(NTK):
                par = i % 2
                if i >= 2:
                    e.wait_ge(s_ev, ev_gate + i - 1)
                for c in range(CCH):
                    inst = e.matmul(pl[par][:],
                                    xn1_sb[:, c * HB + i * 128:c * HB + (i + 1) * 128],
                                    wg_sb[:, c * 8:(c + 1) * 8],
                                    start=(c == 0), stop=(c == CCH - 1))
                inst.then_inc(s_mm, 1)
        em("tensor", f_lg_pe)
        cnt["mm"] += NTK

        def f_lg_v(e, mm_l0=mm_l0):
            for i in range(NTK):
                e.wait_ge(s_mm, mm_l0 + i + 1)
                e.tensor_copy(lgst_sb[:, i * 8:(i + 1) * 8],
                              pl[i % 2][:]).then_inc(s_ev, 1)
            e.memset(junk_v[:], 0.0).then_inc(s_ev, 1)
        em("vector", f_lg_v)
        cnt["ev"] += NTK + 1
        ev_lg = cnt["ev"]

        def f_stlg(e, ev_lg=ev_lg, t0g=t0g):
            e.wait_ge(s_ev, ev_lg)
            e.dma_start(out=lg_d[t0g:t0g + HB, :].rearrange("(i p) e -> p i e", p=128),
                        in_=lgst_sb[:].rearrange("p (i e) -> p i e", e=8)).then_inc(s_st, 16)
        em("gpsimd", f_stlg)
        cnt["st"] += 16

    st_total = cnt["st"]
    em("gpsimd", lambda e: e.wait_ge(s_st, st_total))

    with nc.Block() as block:
        @block.sync
        def _(e):
            for fn in prog["sync"]:
                fn(e)

        @block.tensor
        def _(e):
            for fn in prog["tensor"]:
                fn(e)

        @block.vector
        def _(e):
            for fn in prog["vector"]:
                fn(e)

        @block.scalar
        def _(e):
            for fn in prog["scalar"]:
                fn(e)

        @block.gpsimd
        def _(e):
            for fn in prog["gpsimd"]:
                fn(e)
    return nc


E1 = 8
I1 = 4096
I1CH = I1 // 128
CAP1 = 5120


def build_launch_fin(probe=False):
    nc = bass.Bass("TRN2", target_bir_lowering=False)
    x1_d = nc.dram_tensor("x1_fm", [C, TPC], F32, kind="ExternalInput").ap()
    xn1_d = nc.dram_tensor("xn1_fm", [C, TPC], F32R, kind="ExternalInput").ap()
    ek_d = nc.dram_tensor("EK_fm", [C, TPC], F32, kind="ExternalInput").ap()
    ekv_d = nc.dram_tensor("EKV_fm", [C, TPC], F32, kind="ExternalInput").ap()
    sl_d = nc.dram_tensor("Sloc_fm", [C, TPC], F32, kind="ExternalInput").ap()
    zl_d = nc.dram_tensor("Zloc_fm", [C, TPC], F32, kind="ExternalInput").ap()
    sr_d = nc.dram_tensor("SR_fm", [C, TPC], F32, kind="ExternalInput").ap()
    endsS_d = nc.dram_tensor("endsS", [C, 8], F32, kind="ExternalInput").ap()
    endsZ_d = nc.dram_tensor("endsZ", [C, 8], F32, kind="ExternalInput").ap()
    mrow_d = nc.dram_tensor("mrow", [1, 8], F32R, kind="ExternalInput").ap()
    grow_d = nc.dram_tensor("grow", [1, 8], F32R, kind="ExternalInput").ap()
    W_d = nc.dram_tensor("W_pc", [128, CCH], F32, kind="ExternalInput").ap()
    U_d = nc.dram_tensor("U_pc", [128, CCH], F32, kind="ExternalInput").ap()
    wo_d = nc.dram_tensor("WoT", [C, C], F32R, kind="ExternalInput").ap()
    lgf_d = nc.dram_tensor("logits_full", [N, E1], F32, kind="ExternalInput").ap()
    cfc_d = nc.dram_tensor("c_fc", [E1, C, I1], F32R, kind="ExternalInput").ap()
    cprj_d = nc.dram_tensor("c_proj", [E1, I1, C], F32R, kind="ExternalInput").ap()
    x2_d = nc.dram_tensor("x2_fm", [C, TPC], F32, kind="ExternalOutput").ap()
    gate_dram = nc.dram_tensor("gate_scratch", [N, E1], F32R)

    nrep_d = nc.inline_tensor(
        np.broadcast_to(np.arange(1, TPC + 1, dtype=np.float32), (128, TPC)).copy(),
        name="c_nrep").ap()
    tri_d = nc.inline_tensor(np.triu(np.ones((128, 128), np.float32), 1),
                             name="c_triE").ap()
    allone_d = nc.inline_tensor(np.ones((128, 128), np.float32), name="c_aoE").ap()
    ident_d = nc.inline_tensor(np.eye(128, dtype=np.float32), name="c_identE").ap()
    onesr_d = nc.inline_tensor(np.ones((1, 128), np.float32), name="c_onesrE").ap()
    sel8np = np.zeros((E1, E1 * 128), np.float32)
    for e_ in range(E1):
        sel8np[e_, e_ * 128:(e_ + 1) * 128] = 1.0
    sel8_d = nc.inline_tensor(sel8np, name="c_sel8").ap()

    x1_sb = nc.alloc_sbuf_tensor("x1e_sb", [128, CCH * TPC], F32).ap()
    pc_sb = nc.alloc_sbuf_tensor("pce_sb", [128, 5 * CCH], F32).ap()
    junk_v = nc.alloc_sbuf_tensor("junke_v", [128, 64], F32).ap()
    junk_a = nc.alloc_sbuf_tensor("junke_a", [128, 64], U32).ap()
    afc = [0]
    gfm_sb = nc.alloc_sbuf_tensor("gfm_sb", [E1, TPC], F32R).ap()
    sel8_sb = nc.alloc_sbuf_tensor("sel8_sb", [E1, E1 * 128], F32R).ap()
    onesr_sb = nc.alloc_sbuf_tensor("onesr_sbE", [1, 128], F32R).ap()
    mg_row = nc.alloc_sbuf_tensor("mg_row", [1, 16], F32R).ap()
    mg_sb = nc.alloc_sbuf_tensor("mg_sb", [128, 16], F32).ap()

    pA = [nc.alloc_psum_tensor("pA%d" % i, [128, TC], F32).ap() for i in range(4)]
    pB = nc.alloc_psum_tensor("pB", [128, 16], F32).ap()

    s_ld = nc.alloc_semaphore("e_ld")
    s_lw = nc.alloc_semaphore("e_lw")
    s_lg = nc.alloc_semaphore("e_lg")
    s_lg2 = nc.alloc_semaphore("e_lg2")
    s_lg3 = nc.alloc_semaphore("e_lg3")
    s_mm = nc.alloc_semaphore("e_mm")
    s_ev = nc.alloc_semaphore("e_ev")
    s_act = nc.alloc_semaphore("e_act")
    s_st = nc.alloc_semaphore("e_st")
    s_dch = nc.alloc_semaphore("e_dch")
    s_ach = nc.alloc_semaphore("e_ach")
    cnt = {"ld": 0, "lw": 0, "lg": 0, "mm": 0, "ev": 0, "act": 0, "st": 0}
    dch = [0]
    ach = [0]
    prog = {"sync": [], "tensor": [], "vector": [], "scalar": [], "gpsimd": []}

    def em(eng, fn):
        prog[eng].append(fn)

    def dfence(e):
        dch[0] += 1
        e.memset(junk_v[:, 0:32], 0.0).then_inc(s_dch, 1)
        e.wait_ge(s_dch, dch[0])

    grwkv = nc.sbuf_tensor("rwkv_sb", [128, CCH * TPC], F32R)
    rwkv_sb = grwkv.__enter__().ap()
    gA = []

    def alloc_a(name, shape, dt):
        g = nc.sbuf_tensor(name, shape, dt)
        t = g.__enter__()
        gA.append(g)
        return t.ap()

    ch_sb = alloc_a("ch_sb", [128, 5 * HTC], F32)
    grid_sb = alloc_a("grid_sb", [128, HTC], F32)
    sbufS = alloc_a("sbufS", [128, HTC + 1], F32)
    slast_sb = alloc_a("slast_sb", [128, 2 * CCH], F32)
    sbufZ = alloc_a("sbufZ", [128, HTC + 1], F32)
    tmpN = alloc_a("tmpN", [128, HTC], F32)
    tmpD = alloc_a("tmpD", [128, HTC], F32)
    ends_sb = alloc_a("ends_sb", [128, 2 * CCH * 8], F32)
    cry_sb = alloc_a("cry_sb", [128, 4 * CCH], F32)

    nrep_d = nc.inline_tensor(
        np.broadcast_to(np.arange(1, HTC + 1, dtype=np.float32), (128, HTC)).copy(),
        name="c_nrepE").ap()

    def f_ld0(e):
        e.dma_start(out=pc_sb[:, 0:CCH], in_=W_d).then_inc(s_ld, 16)
        e.dma_start(out=pc_sb[:, CCH:2 * CCH], in_=U_d).then_inc(s_ld, 16)
        for c in range(CCH):
            e.dma_start(out=ends_sb[:, c * 8:(c + 1) * 8],
                        in_=endsS_d[c * 128:(c + 1) * 128, :]).then_inc(s_ld, 16)
            e.dma_start(out=ends_sb[:, (CCH + c) * 8:(CCH + c + 1) * 8],
                        in_=endsZ_d[c * 128:(c + 1) * 128, :]).then_inc(s_ld, 16)
        for c in range(CCH):
            e.dma_start(out=x1_sb[:, c * TPC:(c + 1) * TPC],
                        in_=x1_d[c * 128:(c + 1) * 128, :]).then_inc(s_ld, 16)
    em("sync", f_ld0)
    cnt["ld"] += 16 * (2 + 3 * CCH)
    LD0 = cnt["ld"]

    def f_ldg0(e):
        e.dma_start(out=onesr_sb[:], in_=onesr_d).then_inc(s_lg, 16)
        e.dma_start(out=mg_row[0:1, 0:8], in_=mrow_d).then_inc(s_lg, 16)
        e.dma_start(out=mg_row[0:1, 8:16], in_=grow_d).then_inc(s_lg, 16)
    em("gpsimd", f_ldg0)
    cnt["lg"] += 48
    LG0 = cnt["lg"]

    def f_mg_pe(e):
        e.wait_ge(s_lg, LG0)
        e.matmul(pB[:], onesr_sb[:], mg_row[0:1, :], start=True, stop=True).then_inc(s_mm, 1)
    em("tensor", f_mg_pe)
    cnt["mm"] += 1
    MM_MG = cnt["mm"]

    def f_mg_v(e):
        e.wait_ge(s_mm, MM_MG)
        dfence(e)
        e.tensor_copy(mg_sb[:], pB[:]).then_inc(s_ev, 1)
    em("vector", f_mg_v)
    cnt["ev"] += 1
    EV_MG = cnt["ev"]

    ev_p0 = cnt["ev"]
    act_p0g = cnt["act"]

    def f_prep_act1(e):
        e.wait_ge(s_ld, LD0)
        e.activation(cry_sb[:, 0:CCH], pc_sb[:, 0:CCH], AF.Exp)
        e.activation(pc_sb[:, 3 * CCH:4 * CCH], pc_sb[:, CCH:2 * CCH], AF.Exp)
        afc[0] += 1
        e.memzero(junk_a[:, afc[0] % 64:afc[0] % 64 + 1]).then_inc(s_act, 1)
    em("scalar", f_prep_act1)
    cnt["act"] += 1

    def f_prep_v(e, ev_p0=ev_p0, act_p0g=act_p0g):
        e.wait_ge(s_act, act_p0g + 1)
        e.tensor_scalar(pc_sb[:, 2 * CCH:3 * CCH], cry_sb[:, 0:CCH], -1.0, None,
                        OP.mult)
        e.tensor_scalar(cry_sb[:, CCH:2 * CCH], cry_sb[:, 0:CCH], -1024.0, None,
                        OP.mult)
        e.memset(junk_v[:, 0:32], 0.0).then_inc(s_ev, 1)
    em("vector", f_prep_v)
    cnt["ev"] += 1

    def f_prep_act2(e, ev_p0=ev_p0):
        e.wait_ge(s_ev, ev_p0 + 1)
        e.activation(pc_sb[:, 4 * CCH:5 * CCH], cry_sb[:, CCH:2 * CCH], AF.Exp)
        afc[0] += 1
        e.memzero(junk_a[:, afc[0] % 64:afc[0] % 64 + 1]).then_inc(s_act, 1)
    em("scalar", f_prep_act2)
    cnt["act"] += 1
    ACT_PREP = cnt["act"]

    for j in range(CCH):
        act_c0 = cnt["act"]
        ev_c0 = cnt["ev"]

        def f_cry_v(e, j=j, act_c0=act_c0):
            e.wait_ge(s_ev, EV_MG)
            e.wait_ge(s_act, ACT_PREP)
            dfence(e)
            wcol = pc_sb[:, 2 * CCH + j:2 * CCH + j + 1]
            e.tensor_scalar(tmpN[:, 0:8], mg_sb[:, 0:8], wcol, None, OP.mult)
            e.memset(junk_v[:, 0:32], 0.0).then_inc(s_ev, 1)
            e.wait_ge(s_act, act_c0 + 1)
            e.tensor_mul(tmpN[:, 8:16], tmpN[:, 16:24], mg_sb[:, 8:16])
            dfence(e)
            e.tensor_mul(tmpN[:, 24:32], tmpN[:, 8:16], ends_sb[:, j * 8:(j + 1) * 8])
            e.tensor_mul(tmpN[:, 32:40], tmpN[:, 8:16],
                         ends_sb[:, (CCH + j) * 8:(CCH + j + 1) * 8])
            dfence(e)
            e.tensor_reduce(cry_sb[:, 4 * j:4 * j + 1], tmpN[:, 24:32], AX.X, OP.add)
            e.tensor_reduce(cry_sb[:, 4 * j + 1:4 * j + 2], tmpN[:, 32:40], AX.X, OP.add)
            dfence(e)
            e.tensor_scalar(cry_sb[:, 4 * j + 2:4 * j + 3],
                            cry_sb[:, 4 * j:4 * j + 1],
                            pc_sb[:, 4 * CCH + j:4 * CCH + j + 1], None, OP.mult)
            e.tensor_scalar(cry_sb[:, 4 * j + 3:4 * j + 4],
                            cry_sb[:, 4 * j + 1:4 * j + 2],
                            pc_sb[:, 4 * CCH + j:4 * CCH + j + 1], None, OP.mult)
            dfence(e)
            e.sem_inc(s_ev, 1)
        em("vector", f_cry_v)
        cnt["ev"] += 2

        def f_cry_a(e, j=j, ev_c0=ev_c0):
            e.wait_ge(s_ev, ev_c0 + 1)
            e.activation(tmpN[:, 16:24], tmpN[:, 0:8], AF.Exp)
            afc[0] += 1
            e.memzero(junk_a[:, afc[0] % 64:afc[0] % 64 + 1]).then_inc(s_act, 1)
        em("scalar", f_cry_a)
        cnt["act"] += 1

        ld_g0 = cnt["ld"]
        ev_g0 = cnt["ev"]

        def f_ldnrep(e, ev_g0=ev_g0):
            e.wait_ge(s_ev, ev_g0)
            e.dma_start(out=tmpD[:], in_=nrep_d).then_inc(s_ld, 16)
        em("sync", f_ldnrep)
        cnt["ld"] += 16
        ld_g = cnt["ld"]
        act_gr0 = cnt["act"]

        def f_grid_v(e, j=j, ld_g=ld_g, act_gr0=act_gr0):
            e.wait_ge(s_ld, ld_g)
            e.tensor_scalar(grid_sb[:], tmpD[:],
                            pc_sb[:, 2 * CCH + j:2 * CCH + j + 1], None, OP.mult)
            e.memset(junk_v[:, 0:32], 0.0).then_inc(s_ev, 1)
        em("vector", f_grid_v)
        cnt["ev"] += 1

        def f_grid_a(e, ev_before=cnt["ev"]):
            e.wait_ge(s_ev, ev_before)
            e.activation(grid_sb[:], grid_sb[:], AF.Exp)
            afc[0] += 1
            e.memzero(junk_a[:, afc[0] % 64:afc[0] % 64 + 1]).then_inc(s_act, 1)
        em("scalar", f_grid_a)
        cnt["act"] += 1
        ACT_GRID = cnt["act"]

        for half in range(NHC):
            t0g = half * HTC
            ev_hh = cnt["ev"]

            def f_ldch(e, j=j, t0g=t0g, ev_hh=ev_hh):
                e.wait_ge(s_ev, ev_hh)
                sl = slice(j * 128, (j + 1) * 128)
                ts = slice(t0g, t0g + HTC)
                e.dma_start(out=ch_sb[:, 0:HTC], in_=ek_d[sl, ts]).then_inc(s_ld, 16)
                e.dma_start(out=ch_sb[:, HTC:2 * HTC], in_=ekv_d[sl, ts]).then_inc(s_ld, 16)
                e.dma_start(out=ch_sb[:, 2 * HTC:3 * HTC], in_=sl_d[sl, ts]).then_inc(s_ld, 16)
                e.dma_start(out=ch_sb[:, 3 * HTC:4 * HTC], in_=zl_d[sl, ts]).then_inc(s_ld, 16)
                e.dma_start(out=ch_sb[:, 4 * HTC:5 * HTC], in_=sr_d[sl, ts]).then_inc(s_ld, 16)
            em("sync", f_ldch)
            cnt["ld"] += 16 * 5
            ld_c = cnt["ld"]

            act_h0 = cnt["act"]
            ev_h0 = cnt["ev"]

            def f_att_v(e, j=j, half=half, t0g=t0g, ld_c=ld_c, act_h0=act_h0,
                        ACT_GRID=ACT_GRID):
                e.wait_ge(s_ld, ld_c)
                e.wait_ge(s_act, ACT_GRID)
                cS = cry_sb[:, 4 * j + 2 * half:4 * j + 2 * half + 1]
                cZ = cry_sb[:, 4 * j + 2 * half + 1:4 * j + 2 * half + 2]
                if half == 0:
                    e.tensor_copy(sbufS[:, 0:1], cS)
                    e.tensor_copy(sbufZ[:, 0:1], cZ)
                    e.tensor_copy(slast_sb[:, 2 * j:2 * j + 1],
                                  ch_sb[:, 3 * HTC - 1:3 * HTC])
                    e.tensor_copy(slast_sb[:, 2 * j + 1:2 * j + 2],
                                  ch_sb[:, 4 * HTC - 1:4 * HTC])
                else:
                    e.tensor_add(sbufS[:, 0:1], cS, slast_sb[:, 2 * j:2 * j + 1])
                    e.tensor_add(sbufZ[:, 0:1], cZ, slast_sb[:, 2 * j + 1:2 * j + 2])
                dfence(e)
                e.scalar_tensor_tensor(sbufS[:, 1:HTC + 1], grid_sb[:], cS,
                                       ch_sb[:, 2 * HTC:3 * HTC], OP.mult, OP.add)
                e.scalar_tensor_tensor(sbufZ[:, 1:HTC + 1], grid_sb[:], cZ,
                                       ch_sb[:, 3 * HTC:4 * HTC], OP.mult, OP.add)
                dfence(e)
                ucol = pc_sb[:, 3 * CCH + j:3 * CCH + j + 1]
                e.scalar_tensor_tensor(tmpN[:], ch_sb[:, HTC:2 * HTC], ucol,
                                       sbufS[:, 0:HTC], OP.mult, OP.add)
                e.scalar_tensor_tensor(tmpD[:], ch_sb[:, 0:HTC], ucol,
                                       sbufZ[:, 0:HTC], OP.mult, OP.add)
                e.memset(junk_v[:, 0:32], 0.0).then_inc(s_ev, 1)
                e.wait_ge(s_act, act_h0 + 1)
                e.tensor_mul(tmpN[:], tmpN[:], tmpD[:])
                dfence(e)
                e.tensor_mul(rwkv_sb[:, j * TPC + t0g:j * TPC + t0g + HTC], tmpN[:],
                             ch_sb[:, 4 * HTC:5 * HTC])
                e.memset(junk_v[:, 32:64], 0.0).then_inc(s_ev, 1)
            em("vector", f_att_v)
            cnt["ev"] += 2

            def f_att_a(e, ev_h0=ev_h0):
                e.wait_ge(s_ev, ev_h0 + 1)
                e.activation(tmpD[:], tmpD[:], AF.Ln).then_inc(s_ach, 1)
                ach[0] += 1
                e.wait_ge(s_ach, ach[0])
                e.activation(tmpD[:], tmpD[:], AF.Exp, scale=-1.0)
                afc[0] += 1
                e.memzero(junk_a[:, afc[0] % 64:afc[0] % 64 + 1]).then_inc(s_act, 1)
            em("scalar", f_att_a)
            cnt["act"] += 1

    ev_rwkv_done = cnt["ev"]
    for g in reversed(gA):
        g.__exit__(None, None, None)

    gwm = nc.sbuf_tensor("wmate_sb", [128, CCH * CCH * 128], F32R)
    wmat_sb = gwm.__enter__().ap()
    mm_lw_gate = cnt["mm"]

    def f_ldwo(e, mm_lw_gate=mm_lw_gate, ev_rwkv_done=ev_rwkv_done):
        e.wait_ge(s_mm, mm_lw_gate)
        e.wait_ge(s_ev, ev_rwkv_done)
        e.dma_start(out=wmat_sb[:].rearrange("p (c d j) -> p c d j", c=CCH, d=CCH),
                    in_=wo_d.rearrange("(c p) (d j) -> p c d j", p=128, j=128)
                    ).then_inc(s_lw, 16)
    em("sync", f_ldwo)
    cnt["lw"] += 16
    LW_WO = cnt["lw"]
    mm_wo0 = cnt["mm"]
    NTC4 = TPC // TC
    ev_wo0 = cnt["ev"]

    def f_wo_pe(e, mm_wo0=mm_wo0, ev_rwkv_done=ev_rwkv_done, ev_wo0=ev_wo0):
        e.wait_ge(s_lw, LW_WO)
        e.wait_ge(s_ev, ev_rwkv_done)
        n = 0
        for dj in range(CCH):
            for tc in range(NTC4):
                pb = pA[n % 4]
                if n >= 4:
                    e.wait_ge(s_ev, ev_wo0 + n - 3)
                for c in range(CCH):
                    inst = e.matmul(pb[:],
                                    wmat_sb[:, (c * CCH + dj) * 128:(c * CCH + dj + 1) * 128],
                                    rwkv_sb[:, c * TPC + tc * TC:c * TPC + (tc + 1) * TC],
                                    start=(c == 0), stop=(c == CCH - 1))
                inst.then_inc(s_mm, 1)
                n += 1
    em("tensor", f_wo_pe)
    cnt["mm"] += CCH * NTC4

    def f_wo_v(e, mm_wo0=mm_wo0):
        n = 0
        for dj in range(CCH):
            for tc in range(NTC4):
                e.wait_ge(s_mm, mm_wo0 + n + 1)
                sl = slice(dj * TPC + tc * TC, dj * TPC + (tc + 1) * TC)
                e.tensor_add(x1_sb[:, sl], x1_sb[:, sl], pA[n % 4][:]).then_inc(s_ev, 1)
                n += 1
        e.memset(junk_v[:], 0.0).then_inc(s_ev, 1)
    em("vector", f_wo_v)
    cnt["ev"] += CCH * NTC4 + 1
    EV_ATTN_DONE = cnt["ev"]

    gwm.__exit__(None, None, None)
    grwkv.__exit__(None, None, None)

    gB = []

    def alloc_b(name, shape, dt):
        g = nc.sbuf_tensor(name, shape, dt)
        t = g.__enter__()
        gB.append(g)
        return t.ap()

    lg_sb = alloc_b("lg_sb", [128, N * E1 // 128], F32)
    oh1_sb = alloc_b("oh1_sb", [128, 1024], F32)
    oh2_sb = alloc_b("oh2_sb", [128, 1024], F32)
    m1_sb = alloc_b("m1_sb", [128, 128], F32)
    m2_sb = alloc_b("m2_sb", [128, 128], F32)
    p1_sb = alloc_b("p1_sb", [128, 128], F32)
    p2_sb = alloc_b("p2_sb", [128, 128], F32)
    cum_sb = alloc_b("cum_sb", [128, 1024], F32)
    rs_sb = alloc_b("rs_sb", [128, E1], F32)
    rsr_sb = alloc_b("rsr_sb", [128, E1], F32R)
    xp_sb = alloc_b("xp_sb", [128, 2 * E1], F32)
    gt_sb = alloc_b("gt_sb", [128, 1024], F32R)
    tb_sb = alloc_b("tb_sb", [128, 1024], F32)
    tri_sb = alloc_b("tri_sb", [128, 128], F32R)
    ao_sb = alloc_b("ao_sb", [128, 128], F32R)
    ones8 = alloc_b("ones8", [128, 128], F32)
    idr_sb = alloc_b("idr_sb", [128, 128], F32R)
    gmy_sb = alloc_b("gmy_sb", [128, (TPC // 128) * E1], F32R)

    def f_ldlg(e):
        e.wait_ge(s_ev, EV_ATTN_DONE)
        e.dma_start(out=lg_sb[:], in_=lgf_d.rearrange("(p f) e -> p (f e)", p=128)
                    ).then_inc(s_ld, 16)
    em("sync", f_ldlg)
    cnt["ld"] += 16
    LD_LG = cnt["ld"]

    def f_ldg2(e):
        e.wait_ge(s_ev, EV_ATTN_DONE)
        e.dma_start(out=tri_sb[:], in_=tri_d).then_inc(s_lg2, 16)
        e.dma_start(out=ao_sb[:], in_=allone_d).then_inc(s_lg2, 16)
        e.dma_start(out=idr_sb[:], in_=ident_d).then_inc(s_lg2, 16)
        e.dma_start(out=sel8_sb[:], in_=sel8_d).then_inc(s_lg2, 16)
    em("gpsimd", f_ldg2)
    LG2 = 64

    ev_g0 = cnt["ev"]
    mm_g0 = cnt["mm"]
    act_gb = cnt["act"]

    def f_gate_v(e, ev_g0=ev_g0, mm_g0=mm_g0, act_gb=act_gb):
        e.wait_ge(s_ld, LD_LG)
        e.wait_ge(s_lg2, LG2)
        e.wait_ge(s_ev, EV_ATTN_DONE)
        lg3 = lg_sb[:].rearrange("p (s e) -> p s e", e=E1)
        oh13 = oh1_sb[:].rearrange("p (s e) -> p s e", e=E1)
        oh23 = oh2_sb[:].rearrange("p (s e) -> p s e", e=E1)
        e.memset(ones8[:], 1.0)
        e.tensor_reduce(m1_sb[:], lg3, AX.X, OP.max)
        dfence(e)
        e.tensor_tensor(oh13, lg3, m1_sb[:].unsqueeze(2).broadcast_to([128, 128, E1]),
                        OP.is_ge)
        dfence(e)
        e.tensor_scalar(tb_sb[:], oh1_sb[:], 1e30, None, OP.mult)
        dfence(e)
        e.tensor_sub(tb_sb[:], lg_sb[:], tb_sb[:])
        dfence(e)
        e.tensor_reduce(m2_sb[:], tb_sb[:].rearrange("p (s e) -> p s e", e=E1),
                        AX.X, OP.max)
        dfence(e)
        e.tensor_tensor(oh23, tb_sb[:].rearrange("p (s e) -> p s e", e=E1),
                        m2_sb[:].unsqueeze(2).broadcast_to([128, 128, E1]), OP.is_ge)
        e.tensor_sub(p1_sb[:], m1_sb[:], m2_sb[:])
        e.memset(junk_v[:, 32:64], 0.0).then_inc(s_ev, 1)
        e.tensor_reduce(rs_sb[:], oh1_sb[:].rearrange("p (s e) -> p e s", e=E1),
                        AX.X, OP.add)
        dfence(e)
        e.tensor_copy(rsr_sb[:], rs_sb[:])
        e.memset(junk_v[:, 32:64], 0.0).then_inc(s_ev, 1)
        e.wait_ge(s_mm, mm_g0 + 1)
        dfence(e)
        e.tensor_copy(xp_sb[:, 0:E1], pA[0][:, 0:E1])
        e.tensor_copy(xp_sb[:, E1:2 * E1], pA[1][:, 0:E1])
        dfence(e)
        for ee in range(E1):
            e.tensor_tensor_scan(
                cum_sb[:].rearrange("p (s e) -> p e s", e=E1)[:, ee, :],
                ones8[:],
                oh1_sb[:].rearrange("p (s e) -> p e s", e=E1)[:, ee, :],
                xp_sb[:, ee:ee + 1], OP.mult, OP.add)
        dfence(e)
        e.tensor_sub(cum_sb[:], cum_sb[:], oh1_sb[:])
        dfence(e)
        e.tensor_single_scalar(tb_sb[:], cum_sb[:], float(CAP1), OP.is_lt)
        dfence(e)
        e.tensor_mul(tb_sb[:], tb_sb[:], oh1_sb[:])
        dfence(e)
        e.tensor_reduce(m1_sb[:], tb_sb[:].rearrange("p (s e) -> p s e", e=E1),
                        AX.X, OP.add)
        dfence(e)
        e.wait_ge(s_act, act_gb + 1)
        e.tensor_mul(p1_sb[:], p1_sb[:], m1_sb[:])
        e.tensor_scalar(p2_sb[:], p2_sb[:], 1.0, None, OP.mult)
        e.tensor_reduce(rs_sb[:], oh2_sb[:].rearrange("p (s e) -> p e s", e=E1),
                        AX.X, OP.add)
        dfence(e)
        e.tensor_copy(rsr_sb[:], rs_sb[:])
        e.memset(junk_v[:, 32:64], 0.0).then_inc(s_ev, 1)
        e.wait_ge(s_mm, mm_g0 + 2)
        dfence(e)
        e.tensor_copy(m2_sb[:, 0:E1], pA[2][:, 0:E1])
        dfence(e)
        e.tensor_add(xp_sb[:, 0:E1], m2_sb[:, 0:E1], xp_sb[:, E1:2 * E1])
        dfence(e)
        for ee in range(E1):
            e.tensor_tensor_scan(
                cum_sb[:].rearrange("p (s e) -> p e s", e=E1)[:, ee, :],
                ones8[:],
                oh2_sb[:].rearrange("p (s e) -> p e s", e=E1)[:, ee, :],
                xp_sb[:, ee:ee + 1], OP.mult, OP.add)
        dfence(e)
        e.tensor_sub(cum_sb[:], cum_sb[:], oh2_sb[:])
        dfence(e)
        e.tensor_single_scalar(tb_sb[:], cum_sb[:], float(CAP1), OP.is_lt)
        dfence(e)
        e.tensor_mul(tb_sb[:], tb_sb[:], oh2_sb[:])
        dfence(e)
        e.tensor_reduce(m2_sb[:], tb_sb[:].rearrange("p (s e) -> p s e", e=E1),
                        AX.X, OP.add)
        dfence(e)
        e.tensor_mul(p2_sb[:], p2_sb[:], m2_sb[:])
        dfence(e)
        e.tensor_tensor(tb_sb[:].rearrange("p (s e) -> p s e", e=E1), oh13,
                        p1_sb[:].unsqueeze(2).broadcast_to([128, 128, E1]), OP.mult)
        e.tensor_tensor(gt_sb[:].rearrange("p (s e) -> p s e", e=E1), oh23,
                        p2_sb[:].unsqueeze(2).broadcast_to([128, 128, E1]), OP.mult)
        dfence(e)
        e.tensor_add(gt_sb[:], gt_sb[:], tb_sb[:])
        e.memset(junk_v[:], 0.0).then_inc(s_ev, 1)
    em("vector", f_gate_v)
    cnt["ev"] += 4
    EV_GATE = cnt["ev"]

    def f_gate_a(e, ev_g0=ev_g0):
        e.wait_ge(s_ev, ev_g0 + 1)
        e.activation(p2_sb[:], p1_sb[:], AF.Sigmoid, scale=-1.0).then_inc(s_ach, 1)
        ach[0] += 1
        e.wait_ge(s_ach, ach[0])
        e.activation(p1_sb[:], p1_sb[:], AF.Sigmoid)
        afc[0] += 1
        e.memzero(junk_a[:, afc[0] % 64:afc[0] % 64 + 1]).then_inc(s_act, 1)
    em("scalar", f_gate_a)
    cnt["act"] += 1

    def f_gate_pe(e, ev_g0=ev_g0):
        e.wait_ge(s_ev, ev_g0 + 2)
        e.matmul(pA[0][:, 0:E1], tri_sb[:], rsr_sb[:], start=True, stop=True)
        e.matmul(pA[1][:, 0:E1], ao_sb[:], rsr_sb[:], start=True, stop=True).then_inc(s_mm, 1)
        e.wait_ge(s_ev, ev_g0 + 3)
        e.matmul(pA[2][:, 0:E1], tri_sb[:], rsr_sb[:], start=True, stop=True).then_inc(s_mm, 1)
    em("tensor", f_gate_pe)
    cnt["mm"] += 2
    mm_gate_done = cnt["mm"]

    def f_stgate(e, EV_GATE=EV_GATE):
        e.wait_ge(s_ev, EV_GATE)
        e.dma_start(out=gate_dram.ap().rearrange("(p f) e -> p (f e)", p=128),
                    in_=gt_sb[:]).then_inc(s_st, 16)
    em("gpsimd", f_stgate)
    cnt["st"] += 16
    ST_GATE = cnt["st"]

    def f_ldgmy(e, ST_GATE=ST_GATE):
        e.wait_ge(s_st, ST_GATE)
        pid = e.partition_id()
        gv = gate_dram.ap().rearrange("(r i p) e -> r i p e", r=NCORES, p=128)
        for i in range(TPC // 128):
            e.dma_start(out=gmy_sb[:, i * E1:(i + 1) * E1],
                        in_=gv[pid, i]).then_inc(s_lg3, 16)
    em("gpsimd", f_ldgmy)
    LG_GMY = 16 * (TPC // 128)

    mm_t0 = cnt["mm"]

    def f_gt_pe(e, mm_t0=mm_t0, LG_GMY=LG_GMY, ev_base=cnt["ev"]):
        e.wait_ge(s_lg3, LG_GMY)
        for i in range(TPC // 128):
            par = i % 2
            if i >= 2:
                e.wait_ge(s_ev, ev_base + i - 1)
            e.matmul(pA[par][0:E1, 0:128], gmy_sb[:, i * E1:(i + 1) * E1],
                     idr_sb[:], start=True, stop=True).then_inc(s_mm, 1)
    em("tensor", f_gt_pe)
    cnt["mm"] += TPC // 128

    def f_gt_v(e, mm_t0=mm_t0):
        for i in range(TPC // 128):
            e.wait_ge(s_mm, mm_t0 + i + 1)
            dfence(e)
            e.tensor_copy(gfm_sb[0:E1, i * 128:(i + 1) * 128],
                          pA[i % 2][0:E1, 0:128]).then_inc(s_ev, 1)
        e.memset(junk_v[:, 32:64], 0.0).then_inc(s_ev, 1)
    em("vector", f_gt_v)
    cnt["ev"] += TPC // 128 + 1
    EV_GFM = cnt["ev"]

    for g in reversed(gB):
        g.__exit__(None, None, None)

    gC = []

    def alloc_c(name, shape, dt):
        g = nc.sbuf_tensor(name, shape, dt)
        t = g.__enter__()
        gC.append(g)
        return t.ap()

    xn1h_sb = alloc_c("xn1h_sb", [128, CCH * HTC], F32R)
    oacc_sb = alloc_c("oacc_sb", [128, CCH * HTC], F32)
    sga_sb = alloc_c("sgaE_sb", [128, 4 * TC], F32R)
    cfc_sb = alloc_c("cfc_sb", [128, CCH * 128 * 4], F32R)
    cprj_sb = alloc_c("cprj_sb", [128, 4 * 128 * CCH], F32R)
    crep_sb = alloc_c("crepE_sb", [128, HTC], F32)
    rl_sb = alloc_c("rl_sb", [128, TC], F32)

    IGC = 4
    NIG = I1CH // IGC

    for half in range(NHC):
        t0g = half * HTC
        st_preh = cnt["st"]
        ev_preh = cnt["ev"]

        def f_ldxh(e, t0g=t0g, st_preh=st_preh, ev_preh=ev_preh):
            e.wait_ge(s_ev, ev_preh)
            for c in range(CCH):
                e.dma_start(out=xn1h_sb[:, c * HTC:(c + 1) * HTC],
                            in_=xn1_d[c * 128:(c + 1) * 128, t0g:t0g + HTC]).then_inc(s_ld, 16)
        em("sync", f_ldxh)
        cnt["ld"] += 16 * CCH
        ld_xh = cnt["ld"]

        for ex in range(E1):
            mm_c0 = cnt["mm"]
            ev_c0 = cnt["ev"]

            def f_crep_pe(e, ex=ex, mm_c0=mm_c0, ev_c0=ev_c0, t0g=t0g):
                e.wait_ge(s_ev, ev_c0)
                for tc in range(HTC // TC):
                    e.matmul(pA[tc][:], sel8_sb[:, ex * 128:(ex + 1) * 128],
                             gfm_sb[0:E1, t0g + tc * TC:t0g + (tc + 1) * TC],
                             start=True, stop=True).then_inc(s_mm, 1)
            em("tensor", f_crep_pe)
            cnt["mm"] += HTC // TC

            def f_crep_v(e, mm_c0=mm_c0):
                for tc in range(HTC // TC):
                    e.wait_ge(s_mm, mm_c0 + tc + 1)
                    dfence(e)
                    e.tensor_copy(crep_sb[:, tc * TC:(tc + 1) * TC], pA[tc][:])
                e.memset(oacc_sb[:], 0.0)
                e.memset(junk_v[:], 0.0).then_inc(s_ev, 1)
            em("vector", f_crep_v)
            cnt["ev"] += 1
            ev_crep = cnt["ev"]

            for ig in range(NIG):
                mm_w0 = cnt["mm"]

                def f_ldw(e, ex=ex, ig=ig, mm_w0=mm_w0):
                    e.wait_ge(s_mm, mm_w0)
                    e.dma_start(out=cfc_sb[:].rearrange("p (c i j) -> p c i j", c=CCH, i=IGC),
                                in_=cfc_d[ex][:, ig * 512:(ig + 1) * 512]
                                .rearrange("(c p) (i j) -> p c i j", p=128, j=128)
                                ).then_inc(s_lw, 16)
                    e.dma_start(out=cprj_sb[:].rearrange("p (i h j) -> p i h j", i=IGC, h=CCH),
                                in_=cprj_d[ex][ig * 512:(ig + 1) * 512, :]
                                .rearrange("(i p) (h j) -> p i h j", p=128, j=128)
                                ).then_inc(s_lw, 16)
                em("sync", f_ldw)
                cnt["lw"] += 32
                lw_t = cnt["lw"]

                for tc in range(HTC // TC):
                    mm_g0 = cnt["mm"]
                    act_q0 = cnt["act"]
                    ev_q0 = cnt["ev"]

                    def f_exp_pe(e, ex=ex, ig=ig, tc=tc, lw_t=lw_t, mm_g0=mm_g0,
                                 act_q0=act_q0, ev_q0=ev_q0, ld_xh=ld_xh,
                                 ev_crep=ev_crep):
                        e.wait_ge(s_lw, lw_t)
                        e.wait_ge(s_ld, ld_xh)
                        e.wait_ge(s_ev, ev_crep)
                        for i in range(IGC):
                            if i >= 2:
                                e.wait_ge(s_act, act_q0 + i - 1)
                            else:
                                e.wait_ge(s_act, max(0, act_q0 + i - 2))
                            for c in range(CCH):
                                inst = e.matmul(pA[i % 2][:],
                                                cfc_sb[:, (c * IGC + i) * 128:(c * IGC + i + 1) * 128],
                                                xn1h_sb[:, c * HTC + tc * TC:c * HTC + (tc + 1) * TC],
                                                start=(c == 0), stop=(c == CCH - 1))
                            inst.then_inc(s_mm, 1)
                        e.wait_ge(s_ev, ev_q0 + IGC + 1)
                        for hc in range(CCH):
                            if hc >= 2:
                                e.wait_ge(s_ev, ev_q0 + IGC + 1 + hc - 1)
                            for i in range(IGC):
                                inst = e.matmul(pA[2 + hc % 2][:],
                                                cprj_sb[:, (i * CCH + hc) * 128:(i * CCH + hc + 1) * 128],
                                                sga_sb[:, i * TC:(i + 1) * TC],
                                                start=(i == 0), stop=(i == IGC - 1))
                            inst.then_inc(s_mm, 1)
                    em("tensor", f_exp_pe)
                    cnt["mm"] += IGC + CCH

                    def f_exp_a(e, mm_g0=mm_g0, ev_q0=ev_q0):
                        for i in range(IGC):
                            e.wait_ge(s_mm, mm_g0 + i + 1)
                            if i > 0:
                                e.wait_ge(s_ev, ev_q0 + i)
                            e.activation(rl_sb[:], pA[i % 2][:], AF.Relu).then_inc(s_ach, 1)
                            ach[0] += 1
                            e.wait_ge(s_ach, ach[0])
                            afc[0] += 1
                            e.memzero(junk_a[:, afc[0] % 64:afc[0] % 64 + 1]).then_inc(s_act, 1)
                    em("scalar", f_exp_a)
                    cnt["act"] += IGC

                    def f_exp_v(e, tc=tc, mm_g0=mm_g0, act_q0=act_q0):
                        for i in range(IGC):
                            e.wait_ge(s_act, act_q0 + i + 1)
                            e.tensor_mul(sga_sb[:, i * TC:(i + 1) * TC], rl_sb[:], rl_sb[:]
                                         ).then_inc(s_ev, 1)
                        e.memset(junk_v[:], 0.0).then_inc(s_ev, 1)
                        for hc in range(CCH):
                            e.wait_ge(s_mm, mm_g0 + IGC + hc + 1)
                            sl = slice(hc * HTC + tc * TC, hc * HTC + (tc + 1) * TC)
                            e.tensor_add(oacc_sb[:, sl], oacc_sb[:, sl],
                                         pA[2 + hc % 2][:]).then_inc(s_ev, 1)
                    em("vector", f_exp_v)
                    cnt["ev"] += IGC + 1 + CCH

            ev_oe = cnt["ev"]

            def f_apply_v(e, ev_oe=ev_oe, tcn=HTC // TC, t0g=t0g):
                e.wait_ge(s_ev, ev_oe)
                for hc in range(CCH):
                    for tc in range(tcn):
                        sl = slice(hc * HTC + tc * TC, hc * HTC + (tc + 1) * TC)
                        xsl = slice(hc * TPC + t0g + tc * TC, hc * TPC + t0g + (tc + 1) * TC)
                        e.tensor_mul(rl_sb[:], oacc_sb[:, sl], crep_sb[:, tc * TC:(tc + 1) * TC])
                        dfence(e)
                        e.tensor_add(x1_sb[:, xsl], x1_sb[:, xsl], rl_sb[:])
                        dfence(e)
                e.memset(junk_v[:], 0.0).then_inc(s_ev, 1)
            em("vector", f_apply_v)
            cnt["ev"] += 1


    ev_fin = cnt["ev"]

    def f_stx(e, ev_fin=ev_fin):
        e.wait_ge(s_ev, ev_fin)
        for c in range(CCH):
            e.dma_start(out=x2_d[c * 128:(c + 1) * 128, :],
                        in_=x1_sb[:, c * TPC:(c + 1) * TPC]).then_inc(s_st, 16)
        e.wait_ge(s_st, cnt_st_final[0])
    cnt_st_final = [cnt["st"] + 16 * CCH]
    em("gpsimd", f_stx)
    cnt["st"] += 16 * CCH

    for g in reversed(gC):
        g.__exit__(None, None, None)

    if probe:
        global PROBE
        PROBE = {"gfm": gfm_sb[:], "gmy": gmy_sb[:], "cry": cry_sb[:],
                 "mg": mg_sb[:], "pc": pc_sb[:], "ends": ends_sb[:]}

    with nc.Block() as block:
        @block.sync
        def _(e):
            for fn in prog["sync"]:
                fn(e)

        @block.tensor
        def _(e):
            for fn in prog["tensor"]:
                fn(e)

        @block.vector
        def _(e):
            for fn in prog["vector"]:
                fn(e)

        @block.scalar
        def _(e):
            for fn in prog["scalar"]:
                fn(e)

        @block.gpsimd
        def _(e):
            for fn in prog["gpsimd"]:
                fn(e)
    return nc
